# version 15
# baseline (speedup 1.0000x reference)
import numpy as np
import ml_dtypes
import concourse.bass as bass
import concourse.mybir as mybir
from concourse.bass import IndirectOffsetOnAxis

F32 = mybir.dt.float32
BF16 = mybir.dt.bfloat16
U32 = mybir.dt.uint32
U8 = mybir.dt.uint8
AF = mybir.ActivationFunctionType
ALU = mybir.AluOpType
AX = mybir.AxisListType

ROWS = 2048
SUB = 512
NTC = SUB // 128
D = 4096
FC = 17
H = 11008
HPAD = 11264
HQ = 88
GF = 22
KW1 = 9
KW2 = 23
NBLK = 8
DC = 2112
NBC = 5
EPS = 1e-5


def host_prepare(w1q, a1, b1, w3q, a3, b3, w2q, a2, b2):
    f32 = np.float32
    bf16 = ml_dtypes.bfloat16
    order = [22 * g + c for c in range(22) for g in range(4)]

    def padded_cols(mat_t):
        r = mat_t.shape[0]
        out = np.zeros((r, HPAD), dtype=mat_t.dtype)
        for g in range(4):
            out[:, g * 2816: g * 2816 + 2752] = mat_t[:, g * 2752:(g + 1) * 2752]
        return out

    w1k = np.zeros((KW1 * 128, HPAD), dtype=f32)
    w1k[0:1024] = padded_cols(np.ascontiguousarray(w1q.T).astype(f32, copy=False))
    w1k[1024:1026] = padded_cols(np.ascontiguousarray(b1.T).astype(f32, copy=False))
    w3k = np.zeros((KW1 * 128, HPAD), dtype=f32)
    w3k[0:1024] = padded_cols(np.ascontiguousarray(w3q.T).astype(f32, copy=False))
    w3k[1026:1028] = padded_cols(np.ascontiguousarray(b3.T).astype(f32, copy=False))
    w1r = w1k.reshape(KW1, 128, HQ, 128)[:, :, order, :]
    w3r = w3k.reshape(KW1, 128, HQ, 128)[:, :, order, :]
    wx = np.empty((128, HQ, 2, KW1, 128), dtype=f32)
    wx[:, :, 0] = np.transpose(w1r, (1, 2, 0, 3))
    wx[:, :, 1] = np.transpose(w3r, (1, 2, 0, 3))
    wx = np.ascontiguousarray(wx.reshape(128, HQ * 2 * KW1 * 128))

    w2rows = np.zeros((KW2 * 128, D), dtype=f32)
    w2rows[0:2752] = np.ascontiguousarray(w2q.T).astype(f32, copy=False)
    w2rows[2816:2818] = np.ascontiguousarray(b2.T).astype(f32, copy=False)
    w2r = w2rows.reshape(KW2, 128, NBLK, 512)
    w2x = np.ascontiguousarray(np.transpose(w2r, (1, 2, 0, 3)).reshape(128, NBLK * KW2 * 512))

    a13 = np.concatenate([np.ascontiguousarray(a1.T), np.ascontiguousarray(a3.T)], axis=1).astype(f32, copy=False)
    a13x = np.ascontiguousarray(a13.reshape(32, 128, 4).transpose(1, 0, 2).reshape(128, 128))

    a2p = np.ascontiguousarray(padded_cols(np.ascontiguousarray(a2).astype(f32, copy=False)).T)
    a2x = np.ascontiguousarray(a2p.reshape(HQ, 128, 2)[order].transpose(1, 0, 2).reshape(128, HQ * 2))

    i = np.arange(D, dtype=np.int64)
    k = np.arange(FC * 128, dtype=np.int64)
    ang = ((i[:, None] * k[None, :]) % D).astype(np.float64) * (2.0 * np.pi / D)
    msk = (k <= 2048).astype(np.float64)[None, :]
    Ctab = np.cos(ang) * msk
    Stab = np.sin(ang) * msk

    def tab_layout(T):
        return np.ascontiguousarray(
            T.reshape(32, 128, FC, 128).transpose(1, 2, 0, 3).reshape(128, FC * 32 * 128))

    tabC = tab_layout(Ctab).astype(bf16)
    tabS = tab_layout(Stab).astype(bf16)

    wk = np.ones(FC * 128, dtype=np.float64)
    wk[1:2048] = 2.0
    wk[2049:] = 0.0
    dd = np.arange(DC, dtype=np.int64)
    angd = ((k[:, None] * dd[None, :]) % D).astype(np.float64) * (2.0 * np.pi / D)
    Dt = np.cos(angd) * (wk[:, None] / D)
    Dt[:, 2049:] = 0.0
    tabD = np.ascontiguousarray(Dt.reshape(FC, 128, DC).transpose(1, 0, 2).reshape(128, FC * DC)).astype(bf16)

    rowb = (np.arange(128, dtype=np.uint32) * 8192).reshape(128, 1).copy()
    idn = np.eye(128, dtype=np.float32).astype(bf16)
    return dict(wx=wx, w2x=w2x, a13x=a13x, a2x=a2x,
                tabC=tabC, tabS=tabS, tabD=tabD, rowb=rowb, idn=idn)


class Kb:
    def __init__(self, nc, nsub=4):
        self.nc = nc
        self.nsub = nsub
        self.prog = {e: [] for e in ("pe", "dve", "act", "sp", "pool")}
        self.cnt = {e: 0 for e in self.prog}
        self.sat = {e: {} for e in self.prog}
        self.esem = {}
        self.bufs = {}
        self.state = {}
        self.dsems = {}
        self.semstream = {}
        self._ctx = None

    def init_sems(self, ctx):
        self._ctx = ctx
        for e in self.prog:
            self.esem[e] = ctx.enter_context(self.nc.semaphore(f"sem_{e}"))

    def dstream(self, name):
        if name not in self.dsems:
            sem = self._ctx.enter_context(self.nc.semaphore(f"d_{name}"))
            self.dsems[name] = [sem, 0]
            self.semstream[id(sem)] = name
        return self.dsems[name]

    def reg(self, name, ap, arena, lo, hi):
        self.bufs[name] = (arena, lo, hi)
        self.state[name] = dict(w=None, r=[])
        return ap

    def sb(self, name, shape, dtype):
        t = self._ctx.enter_context(self.nc.sbuf_tensor(name, shape, dtype))
        nbytes = int(np.prod(shape[1:])) * mybir.dt.size(dtype)
        return self.reg(name, t[:], name, 0, nbytes)

    def carve(self, name, arena_name, arena_ap, lo, shape, dtype):
        nbytes = int(np.prod(shape[1:])) * mybir.dt.size(dtype)
        assert lo + nbytes <= arena_ap.shape[1], (name, lo, nbytes, arena_ap.shape)
        view = arena_ap[:, lo:lo + nbytes].bitcast(dtype)
        return self.reg(name, view, arena_name, lo, lo + nbytes)

    def dram(self, name, shape, dtype, kind="Internal"):
        t = self.nc.dram_tensor(name, shape, dtype, kind=kind)
        nbytes = int(np.prod(shape)) * mybir.dt.size(dtype)
        return self.reg(name, t.ap(), "dram_" + name, 0, nbytes)

    def dram_region(self, name, base, lo, hi):
        return self.reg(name, None, "dram_" + base, lo, hi)

    def _conflict_toks(self, reads, writes):
        toks = []
        seen = set()
        items = [(n, False) for n in reads] + [(n, True) for n in writes]
        for name, is_w in items:
            arena, lo, hi = self.bufs[name]
            for other, (ar2, lo2, hi2) in self.bufs.items():
                if ar2 != arena or not (lo < hi2 and lo2 < hi):
                    continue
                st = self.state[other]
                if st["w"] is not None and ("w", other) not in seen:
                    toks.append(st["w"])
                    seen.add(("w", other))
                if is_w:
                    for t in st["r"]:
                        if ("r", other, t) not in seen:
                            toks.append(t)
                            seen.add(("r", other, t))
        return toks

    def _emit_waits(self, eng, toks):
        need = {}
        for (sem, val) in toks:
            sid = id(sem)
            if sid in self.semstream:
                val = self.dsems[self.semstream[sid]][1]
            if self.sat[eng].get(sid, 0) < val:
                prev = need.get(sid)
                need[sid] = (sem, val if prev is None else max(val, prev[1]))
        for sid, (sem, val) in need.items():
            if eng == "pe" and sem is self.esem["pe"]:
                self.sat[eng][sid] = max(self.sat[eng].get(sid, 0), val)
                continue
            self.prog[eng].append(lambda q, s=sem, v=val: q.wait_ge(s, v))
            self.sat[eng][sid] = val

    def _record(self, tok, reads, writes):
        for b in reads:
            self.state[b]["r"].append(tok)
        for b in writes:
            self.state[b] = dict(w=tok, r=[])

    def op(self, eng, fn, reads=(), writes=()):
        toks = self._conflict_toks(reads, writes)
        self._emit_waits(eng, toks)
        sem = self.esem[eng]
        self.cnt[eng] += 1
        v = self.cnt[eng]
        self.prog[eng].append(lambda q, f=fn, s=sem: f(q).then_inc(s, 1))
        tok = (sem, v)
        self._record(tok, reads, writes)
        return tok

    def dma(self, q, stream, fn, reads=(), writes=()):
        toks = self._conflict_toks(reads, writes)
        self._emit_waits(q, toks)
        ds = self.dstream(stream)
        ds[1] += 16
        v = ds[1]
        self.prog[q].append(lambda qq, f=fn, s=ds[0]: f(qq).then_inc(s, 16))
        tok = (ds[0], v)
        self._record(tok, reads, writes)
        return tok

    def finalize_drains(self):
        for name, (sem, cnt) in self.dsems.items():
            if cnt:
                self.prog["sp"].append(lambda q, s=sem, v=cnt: q.wait_ge(s, v))

    def emit(self, block):
        block.tensor(lambda t: [f(t) for f in self.prog["pe"]])
        block.vector(lambda v: [f(v) for f in self.prog["dve"]])
        block.scalar(lambda s: [f(s) for f in self.prog["act"]])
        block.sync(lambda s: [f(s) for f in self.prog["sp"]])
        block.gpsimd(lambda g: [f(g) for f in self.prog["pool"]])


def build_program(nsub=4):
    from contextlib import ExitStack
    nc = bass.Bass("TRN2", target_bir_lowering=False)
    kb = Kb(nc, nsub)

    xin_d = kb.dram("x", [ROWS, D], F32, kind="ExternalInput")
    wx_d = kb.dram("wx", [128, HQ * 2 * KW1 * 128], F32, kind="ExternalInput")
    w2x_d = kb.dram("w2x", [128, NBLK * KW2 * 512], F32, kind="ExternalInput")
    a13_d = kb.dram("a13x", [128, 128], F32, kind="ExternalInput")
    a2_d = kb.dram("a2x", [128, HQ * 2], F32, kind="ExternalInput")
    tabC_d = kb.dram("tabC", [128, FC * 32 * 128], BF16, kind="ExternalInput")
    tabS_d = kb.dram("tabS", [128, FC * 32 * 128], BF16, kind="ExternalInput")
    tabD_d = kb.dram("tabD", [128, FC * DC], BF16, kind="ExternalInput")
    rowb_d = kb.dram("rowb", [128, 1], U32, kind="ExternalInput")
    idn_d = kb.dram("idn", [128, 128], BF16, kind="ExternalInput")
    out_d = kb.dram("out", [ROWS, D], F32, kind="ExternalOutput")
    xn2_d = {(s_, t_): kb.dram(f"xn2_{s_}_{t_}", [128, 8192], BF16, kind="Internal")
             for s_ in range(nsub) for t_ in range(4)}
    wxc_d = kb.dram("wxc", [128, HQ * 2 * KW1 * 128], BF16, kind="Internal")
    w2c_d = kb.dram("w2c", [128, NBLK * KW2 * 512], BF16, kind="Internal")

    with ExitStack() as ctx:
        kb.init_sems(ctx)
        e = ctx.enter_context

        aV = e(nc.sbuf_tensor("aV", [128, 34304], U8))[:]
        aX = e(nc.sbuf_tensor("aX", [128, 32768], U8))[:]
        aW = e(nc.sbuf_tensor("aW", [128, 65536], U8))[:]
        aP = e(nc.sbuf_tensor("aP", [128, 32768], U8))[:]
        aY = e(nc.sbuf_tensor("aY", [128, 32768], U8))[:]

        xnb = [kb.carve(f"xnb{t}", "aV", aV, t * 8192, [128, 4096], BF16) for t in range(4)]
        wqb = [kb.carve(f"wq{j}", "aV", aV, 20480 + j * 4608, [128, 2304], BF16) for j in range(3)]
        gated = [kb.carve(f"gated{g}", "aV", aV, 12288 + g * 1024, [128, 512], BF16) for g in range(8)]

        xnT = kb.carve("xnT", "aX", aX, 0, [128, 32 * 512], BF16)
        w8h = [kb.carve(f"w8h{j}", "aX", aX, j * 6144, [128, 6 * 512], BF16) for j in range(4)]

        xin = [kb.carve("xin0", "aP", aP, 0, [128, 4096], F32),
               kb.carve("xin1", "aP", aP, 16384, [128, 4096], F32)]
        dti = [kb.carve(f"dti{j}", "aW", aW, j * 4224, [128, 2112], BF16) for j in range(12)]
        dti2 = ([kb.carve(f"dti2_{j}", "aV", aV, 4224 + j * 4224, [128, 2112], BF16)
                 for j in range(3)]
                + [kb.carve("dti2_3", "aX", aX, 0, [128, 2112], BF16),
                   kb.carve("dti2_4", "aV", aV, 0, [128, 2112], BF16)])
        s_sl = [[kb.carve(f"s_{t}_{n}", "aW", aW, t * 16384 + n * 2048, [128, 512], F32)
                 for n in range(NBLK)] for t in range(4)]
        s_full = [kb.carve(f"sfull{t}", "aW", aW, t * 16384, [128, 4096], F32) for t in range(4)]

        Pb = kb.carve("P", "aP", aP, 0, [128, FC * 512], BF16)
        xepi = [kb.carve(f"xepi{j}", "aX", aX, 24576 + j * 2048, [128, 512], F32) for j in range(2)]

        tabC_s = [kb.carve(f"tabC{j}", "aY", aY, j * 8192, [128, 32 * 128], BF16) for j in range(2)]
        tabS_s = [kb.carve(f"tabS{j}", "aY", aY, 16384 + j * 8192, [128, 32 * 128], BF16) for j in range(2)]
        corr = [kb.carve(f"corr{t}", "aY", aY, t * 8192, [128, 4096], BF16) for t in range(4)]
        gfold = kb.carve("gfold", "aY", aY, 0, [128, GF * 512], BF16)
        gl2 = kb.carve("gl2", "aY", aY, GF * 1024, [128, 512], BF16)
        gout = [kb.carve("gout0", "aY", aY, 24576, [128, 4096], BF16),
                kb.carve("gout1", "aV", aV, 0, [128, 4096], BF16)]

        xftb = [kb.carve(f"xftb{j}", "aY", aY, 24576 + j * 2048, [128, 1024], BF16) for j in range(4)]
        xfT = kb.sb("xfT", [128, 8 * 512], BF16)
        xl13 = kb.sb("xl13", [128, 512], BF16)
        tmpA = kb.carve("gtmp0", "aV", aV, 0, [128, 512], F32)
        tmpB = kb.carve("gtmp1", "aV", aV, 2048, [128, 512], F32)
        foldt = kb.carve("foldt", "aX", aX, 28672, [128, 512], F32)
        aXj = kb.carve("aXj", "aX", aX, 30720, [128, 512], F32)
        gtmp = [foldt, aXj]
        gtmp_names = ["foldt", "aXj"]
        idb = kb.sb("idb", [128, 128], BF16)
        a13c = kb.sb("a13c", [128, 128], BF16)
        a2c = kb.sb("a2c", [128, HQ * 2], BF16)
        rowb_s = kb.sb("rowb_s", [128, 1], U32)
        max8b = kb.sb("max8b", [128, 8], BF16)
        max8f = kb.sb("max8f", [128, 8], F32)
        d8 = kb.sb("d8", [128, 8], U32)
        idxall = kb.sb("idxall", [128, 32], U32)
        idxa = [kb.reg(f"idxa{t}", idxall[:, t * 8:(t + 1) * 8], "idxall", t * 32, (t + 1) * 32)
                for t in range(4)]
        e8 = kb.sb("e8", [128, 8], F32)
        w8all = kb.sb("w8all", [128, 32], F32)
        w8a = [kb.reg(f"w8a{t}", w8all[:, t * 8:(t + 1) * 8], "w8all", t * 32, (t + 1) * 32)
               for t in range(4)]
        colsT = kb.sb("colsT", [128, 16], F32)
        colsAP = colsT
        def _col(i, name):
            return kb.reg(name, colsAP[:, i:i + 1], "colsT", i * 4, (i + 1) * 4)
        sm1 = _col(0, "sm1")
        sq1 = _col(1, "sq1")
        mcol = _col(2, "mcol")
        mnegl = _col(3, "mnegl")
        vcol = _col(4, "vcol")
        rstd = _col(5, "rstd")
        mbias = _col(6, "mbias")
        mneg = _col(7, "mneg")
        ssum = _col(8, "ssum")
        epsc = _col(9, "epsc")
        srec = _col(10, "srec")
        colsB = kb.sb("colsB", [128, 8], F32)
        colsC = kb.sb("colsC", [128, 8], F32)
        rstd_t = [kb.reg(f"rstd{t}", colsB[:, t:t + 1], "colsB", t * 4, (t + 1) * 4) for t in range(4)]
        mcol_t = [kb.reg(f"mcol{t}", colsB[:, 4 + t:5 + t], "colsB", 16 + t * 4, 20 + t * 4) for t in range(4)]

        pb = [kb.reg(f"pb{i}", e(nc.psum_tensor(f"pb{i}", [128, 512], F32))[:], f"pb{i}", 0, 2048)
              for i in range(4)]
        pbf = kb.reg("pbf", e(nc.psum_tensor("pbf", [128, 512], F32))[:], "pbf", 0, 2048)
        pt2 = kb.reg("pt2", e(nc.psum_tensor("pt2", [2, 512], F32))[:], "pt2", 0, 2048)
        ptrf = [e(nc.psum_tensor(f"ptr{i}", [128, 512], F32))[:] for i in range(2)]
        ptr = [kb.reg(f"ptr{i}", ptrf[i].bitcast(BF16), f"ptr{i}", 0, 2048) for i in range(2)]

        kb.op("dve", lambda q: q.memset(xl13, 0.0), writes=["xl13"])
        kb.op("dve", lambda q: q.memset(gl2, 0.0), writes=["gl2"])
        kb.op("dve", lambda q: q.memset(epsc, EPS), writes=["epsc"])
        kb.dma("sp", "ini_a", lambda q: q.dma_start(out=rowb_s, in_=rowb_d), writes=["rowb_s"])
        kb.dma("sp", "ini_b", lambda q: q.dma_start(out=idb, in_=idn_d), writes=["idb"])
        kb.dma("pool", "ini_c", lambda q: q.dma_start(out=a13c, in_=a13_d), writes=["a13c"])
        kb.dma("pool", "ini_d", lambda q: q.dma_start(out=a2c, in_=a2_d), writes=["a2c"])
        CPW = 4 * 2304
        cast_q = []
        for j in range(22):
            sl = slice(j * CPW, (j + 1) * CPW)
            kb.dram_region(f"wxcp{j}", "wxc", sl.start * 2, sl.stop * 2)
            cast_q.append((lambda q, sl=sl: q.dma_start(out=wxc_d[:, sl], in_=wx_d[:, sl]),
                           f"wxcp{j}"))
        for i in range(NBLK):
            sl = slice(i * KW2 * 512, (i + 1) * KW2 * 512)
            kb.dram_region(f"w2cp{i}", "w2c", sl.start * 2, sl.stop * 2)
            cast_q.append((lambda q, sl=sl: q.dma_start(out=w2c_d[:, sl], in_=w2x_d[:, sl]),
                           f"w2cp{i}"))
        _ci = [0]

        def drip_cast(k):
            while k > 0 and _ci[0] < len(cast_q):
                fn, wname = cast_q[_ci[0]]
                _ci[0] += 1
                k -= 1
                kb.dma("pool", "cast", fn, writes=[wname])

        xnT3 = xnT.rearrange("p (k t) -> p k t", t=512)
        xfT3 = xfT.rearrange("p (k t) -> p k t", t=512)
        Pb3 = Pb.rearrange("p (k t) -> p k t", t=512)
        gf3 = gfold.rearrange("p (k t) -> p k t", t=512)
        wq3 = [w.rearrange("p (a k m) -> p a k m", a=2, m=128) for w in wqb]
        w8h3 = [w.rearrange("p (k t) -> p k t", t=512) for w in w8h]

        stats_done = [[False] * NTC for _ in range(nsub)]
        loads_done = [[False] * NTC for _ in range(nsub)]
        cores_done = [[False] * NTC for _ in range(nsub)]

        def load_task(s_, t_):
            base_ = s_ * SUB
            xslot = f"xin{t_ % 2}"
            kb.dma("pool", xslot, lambda q, r=base_ + t_ * 128, j=t_ % 2: q.dma_start(
                out=xin[j], in_=xin_d[r:r + 128, :]), writes=[xslot])

        def stat_core(s_, t_, junk=None, junk_name=None):
            if junk is None:
                junk, junk_name = tmpA, "gtmp0"
            xslot = f"xin{t_ % 2}"
            xap = xin[t_ % 2]
            kb.op("dve", lambda q, a=xap: q.reduce_sum(sm1, a, axis=AX.X), reads=[xslot], writes=["sm1"])
            for j8 in range(8):
                kb.op("act", lambda q, a=xap, jj=j8, jk=junk: q.activation(
                    jk, a[:, jj * 512:(jj + 1) * 512],
                    AF.Square, accum_out=colsC[:, jj:jj + 1]),
                    reads=[xslot], writes=[junk_name, "colsC"])
            kb.op("dve", lambda q: q.reduce_sum(sq1, colsC, axis=AX.X), reads=["colsC"], writes=["sq1"])
            kb.op("dve", lambda q, tt=t_: q.tensor_scalar(mcol_t[tt], sm1, 1.0 / D, None, op0=ALU.mult),
                  reads=["sm1"], writes=[f"mcol{t_}"])
            kb.op("dve", lambda q, tt=t_: q.tensor_scalar(mnegl, mcol_t[tt], -1.0, None, op0=ALU.mult),
                  reads=[f"mcol{t_}"], writes=["mnegl"])
            kb.op("dve", lambda q: q.tensor_scalar(vcol, sq1, 1.0 / D, None, op0=ALU.mult),
                  reads=["sq1"], writes=["vcol"])
            kb.op("dve", lambda q, tt=t_: q.scalar_tensor_tensor(
                vcol, mcol_t[tt], mnegl, vcol, op0=ALU.mult, op1=ALU.add),
                reads=[f"mcol{t_}", "mnegl", "vcol"], writes=["vcol"])
            kb.op("act", lambda q: q.activation(vcol, vcol, AF.Ln, bias=epsc), reads=["vcol", "epsc"], writes=["vcol"])
            kb.op("act", lambda q, tt=t_: q.activation(rstd_t[tt], vcol, AF.Exp, scale=-0.5),
                  reads=["vcol"], writes=[f"rstd{t_}"])

        def norm_part(s_, t_):
            xslot = f"xin{t_ % 2}"
            xap = xin[t_ % 2]
            for sl4 in range(4):
                kb.op("dve", lambda q, a=xap, o=xnb[t_], tt=t_, ss=sl4: q.tensor_scalar(
                    o[:, ss * 1024:(ss + 1) * 1024], a[:, ss * 1024:(ss + 1) * 1024],
                    mcol_t[tt], rstd_t[tt], op0=ALU.subtract, op1=ALU.mult),
                    reads=[xslot, f"mcol{t_}", f"rstd{t_}"], writes=[f"xnb{t_}"])
            xfb = xftb[t_]
            xfn = f"xftb{t_}"
            kb.op("dve", lambda q, o=xnb[t_], w=xfb: q.tensor_add(w, o[:, 0:1024], o[:, 1024:2048]),
                  reads=[f"xnb{t_}"], writes=[xfn])
            kb.op("dve", lambda q, o=xnb[t_], w=xfb: q.tensor_add(w, w, o[:, 2048:3072]),
                  reads=[f"xnb{t_}", xfn], writes=[xfn])
            kb.op("dve", lambda q, o=xnb[t_], w=xfb: q.tensor_add(w, w, o[:, 3072:4096]),
                  reads=[f"xnb{t_}", xfn], writes=[xfn])
            kb.dma("pool", f"x2w{s_}_{t_}", lambda q, o=xnb[t_], d=xn2_d[(s_, t_)]: q.dma_start(
                out=d[:, 0:4096], in_=o), reads=[f"xnb{t_}"], writes=[f"xn2_{s_}_{t_}"])
            kb.dma("pool", f"x2w{s_}_{t_}", lambda q, o=xnb[t_], d=xn2_d[(s_, t_)]: q.dma_start(
                out=d[:, 4096:8192], in_=o), reads=[f"xnb{t_}"], writes=[f"xn2_{s_}_{t_}"])

        def stat_task(s_, t_):
            stats_done[s_][t_] = True
            load_task(s_, t_)
            stat_core(s_, t_, *((aXj, "aXj") if t_ == 1 else (None, None)))
            norm_part(s_, t_)

        def t_phase(t):
            kb.op("dve", lambda q, tt=t: q.max(out=max8b, in_=corr[tt]), reads=[f"corr{t}"], writes=["max8b"])
            kb.op("dve", lambda q, tt=t: q.max_index(out=d8, in_max=max8b, in_values=corr[tt]),
                  reads=[f"corr{t}", "max8b"], writes=["d8"])
            kb.op("dve", lambda q: q.tensor_copy(out=max8f, in_=max8b), reads=["max8b"], writes=["max8f"])
            kb.op("dve", lambda q: q.tensor_scalar(mneg, max8f[:, 0:1], -1.0, None, op0=ALU.mult),
                  reads=["max8f"], writes=["mneg"])
            kb.op("act", lambda q: q.activation(e8, max8f, AF.Exp, bias=mneg),
                  reads=["max8f", "mneg"], writes=["e8"])
            kb.op("dve", lambda q: q.reduce_sum(ssum, e8, axis=AX.X), reads=["e8"], writes=["ssum"])
            kb.op("dve", lambda q: q.reciprocal(srec, ssum), reads=["ssum"], writes=["srec"])
            kb.op("dve", lambda q, tt=t: q.tensor_scalar(w8a[tt], e8, srec, None, op0=ALU.mult),
                  reads=["e8", "srec"], writes=[f"w8a{t}"])
            kb.op("dve", lambda q, tt=t: q.tensor_tensor(
                idxa[tt], d8, rowb_s[:, 0:1].to_broadcast([128, 8]), op=ALU.add),
                reads=["d8", "rowb_s"], writes=[f"idxa{t}"])

        for s in range(nsub):
            base = s * SUB
            for t in range(NTC):
                if not stats_done[s][t]:
                    if not loads_done[s][t]:
                        load_task(s, t)
                    if not cores_done[s][t]:
                        stat_core(s, t)
                    norm_part(s, t)
                    stats_done[s][t] = True
                for b in range(4):
                    pslot, pap = f"ptr{b % 2}", ptr[b % 2]
                    for j in range(8):
                        fcn = b * 8 + j
                        kb.op("pe", lambda q, src=xnb[t], f=fcn, o=pap, jj=j: q.transpose(
                            o[:, jj * 128:(jj + 1) * 128], src[:, f * 128:(f + 1) * 128], idb),
                            reads=[f"xnb{t}", "idb"], writes=[pslot])
                    dst = xnT3[:, b * 8:(b + 1) * 8, t * 128:(t + 1) * 128]
                    kb.op("act", lambda q, o=dst, i=pap: q.copy(o, i.rearrange("p (j m) -> p j m", m=128)),
                          reads=[pslot], writes=["xnT"])
                for j in range(8):
                    kb.op("pe", lambda q, f=j, o=ptr[0], jj=j, w=xftb[t]: q.transpose(
                        o[:, jj * 128:(jj + 1) * 128], w[:, f * 128:(f + 1) * 128], idb),
                        reads=[f"xftb{t}", "idb"], writes=["ptr0"])
                dst = xfT3[:, :, t * 128:(t + 1) * 128]
                kb.op("act", lambda q, o=dst, i=ptr[0]: q.copy(o, i.rearrange("p (j m) -> p j m", m=128)),
                      reads=["ptr0"], writes=["xfT"])

            for j in range(12):
                kb.dma("act", "dgA", lambda q, k=j: q.dma_start(
                    out=dti[k], in_=tabD_d[:, k * DC:(k + 1) * DC]), writes=[f"dti{j}"])
            drip_cast(6)
            for j in range(3):
                kb.dma("pool", "dgB", lambda q, k=j: q.dma_start(
                    out=dti2[k], in_=tabD_d[:, (12 + k) * DC:(13 + k) * DC]), writes=[f"dti2_{j}"])
            for kc in range(32):
                kb.op("pe", lambda q, k=kc: q.matmul(pbf[0:4, :], lhsT=a13c[:, k * 4:k * 4 + 4],
                                                     rhs=xnT[:, k * 512:(k + 1) * 512],
                                                     start=(k == 0), stop=(k == 31)),
                      reads=["a13c", "xnT"], writes=["pbf"])
            kb.op("act", lambda q: q.mul(xl13[0:4, :], pbf[0:4, :], 0.5), reads=["pbf"], writes=["xl13"])

            kb.dma("sp", "tabC0", lambda q: q.dma_start(out=tabC_s[0], in_=tabC_d[:, 0:4096]), writes=["tabC0"])
            kb.dma("sp", "tabS0", lambda q: q.dma_start(out=tabS_s[0], in_=tabS_d[:, 0:4096]), writes=["tabS0"])
            for fc in range(FC):
                if fc + 1 < FC:
                    j = (fc + 1) % 2
                    kb.dma("sp", f"tabC{j}", lambda q, f=fc + 1, jj=j: q.dma_start(
                        out=tabC_s[jj], in_=tabC_d[:, f * 4096:(f + 1) * 4096]), writes=[f"tabC{j}"])
                    kb.dma("sp", f"tabS{j}", lambda q, f=fc + 1, jj=j: q.dma_start(
                        out=tabS_s[jj], in_=tabS_d[:, f * 4096:(f + 1) * 4096]), writes=[f"tabS{j}"])
                cslot, sslot = f"tabC{fc % 2}", f"tabS{fc % 2}"
                cap, sap = tabC_s[fc % 2], tabS_s[fc % 2]
                rb, ib = f"pb{fc % 2}", f"pb{2 + fc % 2}"
                rap, iap = pb[fc % 2], pb[2 + fc % 2]
                for kc in range(32):
                    kb.op("pe", lambda q, k=kc, c=cap, o=rap: q.matmul(
                        o, lhsT=c[:, k * 128:(k + 1) * 128], rhs=xnT[:, k * 512:(k + 1) * 512],
                        start=(k == 0), stop=(k == 31)), reads=[cslot, "xnT"], writes=[rb])
                for kc in range(32):
                    kb.op("pe", lambda q, k=kc, c=sap, o=iap: q.matmul(
                        o, lhsT=c[:, k * 128:(k + 1) * 128], rhs=xnT[:, k * 512:(k + 1) * 512],
                        start=(k == 0), stop=(k == 31)), reads=[sslot, "xnT"], writes=[ib])
                kb.op("act", lambda q, i=rap: q.activation(tmpA, i, AF.Square), reads=[rb], writes=["gtmp0"])
                kb.op("act", lambda q, i=iap: q.activation(tmpB, i, AF.Square), reads=[ib], writes=["gtmp1"])
                kb.op("dve", lambda q, f=fc: q.tensor_add(Pb[:, f * 512:(f + 1) * 512], tmpA, tmpB),
                      reads=["gtmp0", "gtmp1"], writes=["P"])
            for j in range(3, 5):
                kb.dma("pool", "dgB", lambda q, k=j: q.dma_start(
                    out=dti2[k], in_=tabD_d[:, (12 + k) * DC:(13 + k) * DC]), writes=[f"dti2_{j}"])

            for j0 in range(2):
                kb.dma("sp", f"wq{j0}", lambda q, i=j0: q.dma_start(
                    out=wqb[i], in_=wxc_d[:, i * 2304:(i + 1) * 2304]),
                    reads=["wxcp0"], writes=[f"wq{j0}"])
            for t in range(NTC):
                for n in range(NBC):
                    bank = (t * NBC + n) % 4
                    bap, bname = pb[bank], f"pb{bank}"
                    nw = 512 if n < 4 else 1
                    for kc in range(FC):
                        dd = dti[kc] if kc < 12 else dti2[kc - 12]
                        dname = f"dti{kc}" if kc < 12 else f"dti2_{kc - 12}"
                        kb.op("pe", lambda q, k=kc, tt=t, nn=n, o=bap, w=nw, d_=dd: q.matmul(
                            o[:, 0:w], lhsT=Pb3[:, k, tt * 128:(tt + 1) * 128],
                            rhs=d_[:, nn * 512:nn * 512 + w],
                            start=(k == 0), stop=(k == FC - 1)),
                            reads=["P", dname], writes=[bname])
                    kb.op("act", lambda q, tt=t, nn=n, i=bap, w=nw: q.copy(
                        corr[tt][:, nn * 512:nn * 512 + w], i[:, 0:w]),
                        reads=[bname], writes=[f"corr{t}"])
                kb.op("dve", lambda q, tt=t: q.tensor_copy(
                    out=corr[tt][:, 2049:4096], in_=corr[tt][:, 2047:0:-1]),
                    reads=[f"corr{t}"], writes=[f"corr{t}"])
                t_phase(t)
                drip_cast(1)

            def mac_task(t, j):
                gslot, gap = f"gout{(t * 8 + j) % 2}", gout[(t * 8 + j) % 2]
                kb.dma("pool", f"g{(t * 8 + j) % 2}", lambda q, jj=j, o=gap, d=xn2_d[(s, t)], tt=t:
                       q.indirect_dma_start(
                           out=o, out_offset=None, in_=d,
                           in_offset=IndirectOffsetOnAxis(idxa[tt][:, jj:jj + 1], axis=1)),
                       reads=[f"idxa{t}", f"xn2_{s}_{t}"], writes=[gslot])
                sname = f"sfull{t}"
                if j == 0:
                    kb.op("dve", lambda q, g=gap, tt=t: q.tensor_scalar(
                        s_full[tt], g, w8a[tt][:, 0:1], None, op0=ALU.mult),
                        reads=[gslot, f"w8a{t}"], writes=[sname])
                else:
                    kb.op("dve", lambda q, g=gap, jj=j, tt=t: q.scalar_tensor_tensor(
                        s_full[tt], g, w8a[tt][:, jj:jj + 1], s_full[tt],
                        op0=ALU.mult, op1=ALU.add),
                        reads=[gslot, f"w8a{t}", sname], writes=[sname])

            mac_tasks = [(t, j) for t in range(NTC) for j in range(8)]
            mac_i = 0

            def emit_t2_fold(cc):
                for g in range(4):
                    qi = cc * 4 + g
                    gs = 4 * (cc % 2) + g
                    kb.op("pe", lambda q, i=qi, gg=gs: q.matmul(
                        pt2[:], lhsT=a2c[:, i * 2:(i + 1) * 2], rhs=gated[gg],
                        start=(i == 0), stop=(i == HQ - 1)),
                        reads=["a2c", f"gated{gs}"], writes=["pt2"])
                g0, g1, g2, g3 = (4 * (cc % 2) + g for g in range(4))
                kb.op("dve", lambda q, a=gated[g0], b=gated[g1]: q.tensor_add(foldt, a, b),
                      reads=[f"gated{g0}", f"gated{g1}"], writes=["foldt"])
                kb.op("dve", lambda q, a=gated[g2]: q.tensor_add(foldt, foldt, a),
                      reads=[f"gated{g2}", "foldt"], writes=["foldt"])
                kb.op("dve", lambda q, a=gated[g3], cc=cc: q.tensor_add(
                    gfold[:, cc * 512:(cc + 1) * 512], foldt, a),
                    reads=[f"gated{g3}", "foldt"], writes=["gfold"])

            for c in range(GF):
                for g in range(4):
                    qi = c * 4 + g
                    slot = qi % 3
                    gs = 4 * (c % 2) + g
                    if qi + 2 < HQ:
                        ns = (qi + 2) % 3
                        kb.dma("sp", f"wq{ns}", lambda q, i=qi + 2, jj=ns: q.dma_start(
                            out=wqb[jj], in_=wxc_d[:, i * 2304:(i + 1) * 2304]),
                            reads=[f"wxcp{(qi + 2) // 4}"], writes=[f"wq{ns}"])
                    h1n, h3n = f"pb{2 * (qi % 2)}", f"pb{2 * (qi % 2) + 1}"
                    h1ap, h3ap = pb[2 * (qi % 2)], pb[2 * (qi % 2) + 1]
                    wslot, wap = f"wq{slot}", wq3[slot]
                    for kc in range(KW1):
                        rhs = xfT[:, kc * 512:(kc + 1) * 512] if kc < 8 else xl13
                        rname = "xfT" if kc < 8 else "xl13"
                        kb.op("pe", lambda q, k=kc, w=wap, o=h1ap, r=rhs: q.matmul(
                            o, lhsT=w[:, 0, k, :], rhs=r, start=(k == 0), stop=(k == KW1 - 1)),
                            reads=[wslot, rname], writes=[h1n])
                    for kc in range(KW1):
                        rhs = xfT[:, kc * 512:(kc + 1) * 512] if kc < 8 else xl13
                        rname = "xfT" if kc < 8 else "xl13"
                        kb.op("pe", lambda q, k=kc, w=wap, o=h3ap, r=rhs: q.matmul(
                            o, lhsT=w[:, 1, k, :], rhs=r, start=(k == 0), stop=(k == KW1 - 1)),
                            reads=[wslot, rname], writes=[h3n])
                    if g == 3 and c >= 1:
                        emit_t2_fold(c - 1)
                    kb.op("act", lambda q, i=h1ap, o=gated[gs]: q.activation(o, i, AF.Silu),
                          reads=[h1n], writes=[f"gated{gs}"])
                    kb.op("dve", lambda q, gg=gs, hp=h3ap: q.tensor_mul(gated[gg], gated[gg], hp),
                          reads=[f"gated{gs}", h3n], writes=[f"gated{gs}"])
                    if c >= 2 and g % 2 == 1 and mac_i < len(mac_tasks):
                        mac_task(*mac_tasks[mac_i])
                        mac_i += 1
                    if g == 3:
                        drip_cast(2)
                    if g == 3 and c in (19, 21) and s + 1 < nsub and not stats_done[s + 1][0 if c == 19 else 1]:
                        stat_task(s + 1, 0 if c == 19 else 1)
            emit_t2_fold(GF - 1)
            while mac_i < len(mac_tasks):
                mac_task(*mac_tasks[mac_i])
                mac_i += 1
            kb.op("act", lambda q: q.mul(gl2[0:2, :], pt2[:], 0.5), reads=["pt2"], writes=["gl2"])

            def w2piece(n, p):
                off = n * KW2 * 512 + 6 * p * 512
                ln = (6 if p < 3 else 5) * 512
                return off, ln
            pieces = [(n, p) for n in range(NBLK) for p in range(4)]
            for pre in range(3):
                n2, p2 = pieces[pre]
                off, ln = w2piece(n2, p2)
                kb.dma("act", f"w8h{pre}", lambda q, o=off, l=ln, jj=pre: q.dma_start(
                    out=w8h[jj][:, 0:l], in_=w2c_d[:, o:o + l]), reads=[f"w2cp{n2}"], writes=[f"w8h{pre}"])
            ep = 0
            for pi, (n, p) in enumerate(pieces):
                slot = pi % 4
                if pi + 3 < len(pieces):
                    n2, p2 = pieces[pi + 3]
                    off, ln = w2piece(n2, p2)
                    ns = (pi + 3) % 4
                    kb.dma("act", f"w8h{ns}", lambda q, o=off, l=ln, jj=ns: q.dma_start(
                        out=w8h[jj][:, 0:l], in_=w2c_d[:, o:o + l]), reads=[f"w2cp{n2}"], writes=[f"w8h{ns}"])
                kcs = range(6 * p, min(6 * p + 6, KW2))
                for t in range(NTC):
                    bank = (n * 4 + t) % 5
                    bap, bname = (pb + [pbf])[bank], (["pb0", "pb1", "pb2", "pb3", "pbf"])[bank]
                    for kc in kcs:
                        if kc < GF:
                            lhs, lname = gf3[:, kc, t * 128:(t + 1) * 128], "gfold"
                        else:
                            lhs, lname = gl2[:, t * 128:(t + 1) * 128], "gl2"
                        ko = kc - 6 * p
                        kb.op("pe", lambda q, k=ko, o=bap, l=lhs, w=w8h3[slot], st=(kc == 0), sp=(kc == KW2 - 1): q.matmul(
                            o, lhsT=l, rhs=w[:, k, :], start=st, stop=sp),
                            reads=[lname, f"w8h{slot}"], writes=[bname])
                if p == 3:
                    if s + 1 < nsub:
                        if n == 0 and not loads_done[s + 1][2]:
                            loads_done[s + 1][2] = True
                            load_task(s + 1, 2)
                        if n == 1 and not cores_done[s + 1][2]:
                            cores_done[s + 1][2] = True
                            stat_core(s + 1, 2, aXj, "aXj")
                        if n == 2 and not stats_done[s + 1][2]:
                            stats_done[s + 1][2] = True
                            norm_part(s + 1, 2)
                            if not loads_done[s + 1][3]:
                                loads_done[s + 1][3] = True
                                load_task(s + 1, 3)
                        if n == 3 and not cores_done[s + 1][3]:
                            cores_done[s + 1][3] = True
                            stat_core(s + 1, 3, aXj, "aXj")
                        if n == 4 and not stats_done[s + 1][3]:
                            stats_done[s + 1][3] = True
                            norm_part(s + 1, 3)
                    for t in range(NTC):
                        bank = (n * 4 + t) % 5
                        bap, bname = (pb + [pbf])[bank], (["pb0", "pb1", "pb2", "pb3", "pbf"])[bank]
                        gsl, gapp = gtmp_names[ep % 2], gtmp[ep % 2]
                        kb.op("act", lambda q, i=bap, o=gapp: q.activation(o, i, AF.Silu),
                              reads=[bname], writes=[gsl])
                        xs, xa = f"xepi{ep % 2}", xepi[ep % 2]
                        r0 = base + t * 128
                        kb.dma("pool", xs, lambda q, r=r0, nn=n, o=xa: q.dma_start(
                            out=o, in_=xin_d[r:r + 128, nn * 512:(nn + 1) * 512]), writes=[xs])
                        sname = f"s_{t}_{n}"
                        kb.op("dve", lambda q, tt=t, nn=n, g=gapp: q.tensor_mul(s_sl[tt][nn], s_sl[tt][nn], g),
                              reads=[sname, gsl], writes=[sname])
                        kb.op("dve", lambda q, tt=t, nn=n, x=xa: q.tensor_add(s_sl[tt][nn], s_sl[tt][nn], x),
                              reads=[sname, xs], writes=[sname])
                        kb.dma("pool", "outw", lambda q, r=r0, tt=t, nn=n: q.dma_start(
                            out=out_d[r:r + 128, nn * 512:(nn + 1) * 512], in_=s_sl[tt][nn]),
                            reads=[sname])
                        ep += 1

        kb.finalize_drains()
        block = e(nc.Block())
        kb.emit(block)
    return nc


from concourse.bass_utils import run_bass_kernel_spmd

_NCORES = 8


def kernel(L0_attn_x, L0_ffn_w1q, L0_ffn_a1, L0_ffn_b1, L0_ffn_w3q, L0_ffn_a3,
           L0_ffn_b3, L0_ffn_w2q, L0_ffn_a2, L0_ffn_b2):
    x = np.ascontiguousarray(np.asarray(L0_attn_x, dtype=np.float32).reshape(16384, D))
    hp = host_prepare(L0_ffn_w1q, L0_ffn_a1, L0_ffn_b1, L0_ffn_w3q, L0_ffn_a3,
                      L0_ffn_b3, L0_ffn_w2q, L0_ffn_a2, L0_ffn_b2)
    shards = np.split(x, _NCORES, axis=0)
    in_maps = [dict(x=shards[i], **hp) for i in range(_NCORES)]
    nc = build_program(nsub=4)
    res = run_bass_kernel_spmd(nc, in_maps, core_ids=list(range(_NCORES)))
    out = np.concatenate([res.results[i]["out"] for i in range(_NCORES)], axis=0)
    return out.reshape(2, 8192, D)


# version 16
# speedup vs baseline: 1.0492x; 1.0492x over previous
import numpy as np
import ml_dtypes
import concourse.bass as bass
import concourse.mybir as mybir
from concourse.bass import IndirectOffsetOnAxis

F32 = mybir.dt.float32
BF16 = mybir.dt.bfloat16
U32 = mybir.dt.uint32
U8 = mybir.dt.uint8
AF = mybir.ActivationFunctionType
ALU = mybir.AluOpType
AX = mybir.AxisListType

ROWS = 2048
SUB = 512
NTC = SUB // 128
D = 4096
FC = 17
H = 11008
HPAD = 11264
HQ = 88
GF = 22
KW1 = 9
KW2 = 23
NBLK = 8
DC = 2112
NBC = 5
EPS = 1e-5


def host_prepare(w1q, a1, b1, w3q, a3, b3, w2q, a2, b2):
    f32 = np.float32
    bf16 = ml_dtypes.bfloat16
    order = [22 * g + c for c in range(22) for g in range(4)]

    def padded_cols(mat_t):
        r = mat_t.shape[0]
        out = np.zeros((r, HPAD), dtype=mat_t.dtype)
        for g in range(4):
            out[:, g * 2816: g * 2816 + 2752] = mat_t[:, g * 2752:(g + 1) * 2752]
        return out

    w1k = np.zeros((KW1 * 128, HPAD), dtype=f32)
    w1k[0:1024] = padded_cols(np.ascontiguousarray(w1q.T).astype(f32, copy=False))
    w1k[1024:1026] = padded_cols(np.ascontiguousarray(b1.T).astype(f32, copy=False))
    w3k = np.zeros((KW1 * 128, HPAD), dtype=f32)
    w3k[0:1024] = padded_cols(np.ascontiguousarray(w3q.T).astype(f32, copy=False))
    w3k[1026:1028] = padded_cols(np.ascontiguousarray(b3.T).astype(f32, copy=False))
    w1r = w1k.reshape(KW1, 128, HQ, 128)[:, :, order, :]
    w3r = w3k.reshape(KW1, 128, HQ, 128)[:, :, order, :]
    wx = np.empty((128, HQ, 2, KW1, 128), dtype=f32)
    wx[:, :, 0] = np.transpose(w1r, (1, 2, 0, 3))
    wx[:, :, 1] = np.transpose(w3r, (1, 2, 0, 3))
    wx = np.ascontiguousarray(wx.reshape(128, HQ * 2 * KW1 * 128))

    w2rows = np.zeros((KW2 * 128, D), dtype=f32)
    w2rows[0:2752] = np.ascontiguousarray(w2q.T).astype(f32, copy=False)
    w2rows[2816:2818] = np.ascontiguousarray(b2.T).astype(f32, copy=False)
    w2r = w2rows.reshape(KW2, 128, NBLK, 512)
    w2x = np.ascontiguousarray(np.transpose(w2r, (1, 2, 0, 3)).reshape(128, NBLK * KW2 * 512))

    a13 = np.concatenate([np.ascontiguousarray(a1.T), np.ascontiguousarray(a3.T)], axis=1).astype(f32, copy=False)
    jj = np.arange(2047)
    Ac = np.zeros((2176, 4), dtype=f32)
    As = np.zeros((2048, 4), dtype=f32)
    Ac[jj] = (a13[jj + 1] + a13[4095 - jj]) * 0.5
    Ac[2047] = a13[0]
    Ac[2048] = a13[2048]
    As[jj] = (a13[jj + 1] - a13[4095 - jj]) * 0.5
    acs = np.concatenate([Ac, As], axis=0)
    a13x = np.ascontiguousarray(acs.reshape(33, 128, 4).transpose(1, 0, 2).reshape(128, 132))

    a2p = np.ascontiguousarray(padded_cols(np.ascontiguousarray(a2).astype(f32, copy=False)).T)
    a2x = np.ascontiguousarray(a2p.reshape(HQ, 128, 2)[order].transpose(1, 0, 2).reshape(128, HQ * 2))

    k = np.arange(FC * 128, dtype=np.int64)
    msk = (k <= 2048).astype(np.float64)[None, :]
    ii = (jj + 1).astype(np.int64)
    angf = ((ii[:, None] * k[None, :]) % D).astype(np.float64) * (2.0 * np.pi / D)
    Crows = np.zeros((2176, FC * 128))
    Crows[jj] = np.cos(angf)
    Crows[2047] = 1.0
    Crows[2048] = np.where(k % 2 == 0, 1.0, -1.0)
    Crows *= msk
    Srows = np.zeros((2048, FC * 128))
    Srows[jj] = np.sin(angf)
    Srows *= msk

    tabC = np.ascontiguousarray(
        Crows.reshape(17, 128, FC, 128).transpose(1, 2, 0, 3).reshape(128, FC * 17 * 128)).astype(bf16)
    tabS = np.ascontiguousarray(
        Srows.reshape(16, 128, FC, 128).transpose(1, 2, 0, 3).reshape(128, FC * 16 * 128)).astype(bf16)

    wk = np.ones(FC * 128, dtype=np.float64)
    wk[1:2048] = 2.0
    wk[2049:] = 0.0
    dd = np.arange(DC, dtype=np.int64)
    angd = ((k[:, None] * dd[None, :]) % D).astype(np.float64) * (2.0 * np.pi / D)
    Dt = np.cos(angd) * (wk[:, None] / D)
    Dt[:, 2049:] = 0.0
    tabD = np.ascontiguousarray(Dt.reshape(FC, 128, DC).transpose(1, 0, 2).reshape(128, FC * DC)).astype(bf16)

    rowb = (np.arange(128, dtype=np.uint32) * 8192).reshape(128, 1).copy()
    idn = np.eye(128, dtype=np.float32).astype(bf16)
    return dict(wx=wx, w2x=w2x, a13x=a13x, a2x=a2x,
                tabC=tabC, tabS=tabS, tabD=tabD, rowb=rowb, idn=idn)


class Kb:
    def __init__(self, nc, nsub=4):
        self.nc = nc
        self.nsub = nsub
        self.prog = {e: [] for e in ("pe", "dve", "act", "sp", "pool")}
        self.cnt = {e: 0 for e in self.prog}
        self.sat = {e: {} for e in self.prog}
        self.esem = {}
        self.bufs = {}
        self.state = {}
        self.dsems = {}
        self.semstream = {}
        self._ctx = None

    def init_sems(self, ctx):
        self._ctx = ctx
        for e in self.prog:
            self.esem[e] = ctx.enter_context(self.nc.semaphore(f"sem_{e}"))

    def dstream(self, name):
        if name not in self.dsems:
            sem = self._ctx.enter_context(self.nc.semaphore(f"d_{name}"))
            self.dsems[name] = [sem, 0]
            self.semstream[id(sem)] = name
        return self.dsems[name]

    def reg(self, name, ap, arena, lo, hi):
        self.bufs[name] = (arena, lo, hi)
        self.state[name] = dict(w=None, r=[])
        return ap

    def sb(self, name, shape, dtype):
        t = self._ctx.enter_context(self.nc.sbuf_tensor(name, shape, dtype))
        nbytes = int(np.prod(shape[1:])) * mybir.dt.size(dtype)
        return self.reg(name, t[:], name, 0, nbytes)

    def carve(self, name, arena_name, arena_ap, lo, shape, dtype):
        nbytes = int(np.prod(shape[1:])) * mybir.dt.size(dtype)
        assert lo + nbytes <= arena_ap.shape[1], (name, lo, nbytes, arena_ap.shape)
        view = arena_ap[:, lo:lo + nbytes].bitcast(dtype)
        return self.reg(name, view, arena_name, lo, lo + nbytes)

    def dram(self, name, shape, dtype, kind="Internal"):
        t = self.nc.dram_tensor(name, shape, dtype, kind=kind)
        nbytes = int(np.prod(shape)) * mybir.dt.size(dtype)
        return self.reg(name, t.ap(), "dram_" + name, 0, nbytes)

    def dram_region(self, name, base, lo, hi):
        return self.reg(name, None, "dram_" + base, lo, hi)

    def _conflict_toks(self, reads, writes):
        toks = []
        seen = set()
        items = [(n, False) for n in reads] + [(n, True) for n in writes]
        for name, is_w in items:
            arena, lo, hi = self.bufs[name]
            for other, (ar2, lo2, hi2) in self.bufs.items():
                if ar2 != arena or not (lo < hi2 and lo2 < hi):
                    continue
                st = self.state[other]
                if st["w"] is not None and ("w", other) not in seen:
                    toks.append(st["w"])
                    seen.add(("w", other))
                if is_w:
                    for t in st["r"]:
                        if ("r", other, t) not in seen:
                            toks.append(t)
                            seen.add(("r", other, t))
        return toks

    def _emit_waits(self, eng, toks):
        need = {}
        for (sem, val) in toks:
            sid = id(sem)
            if sid in self.semstream:
                val = self.dsems[self.semstream[sid]][1]
            if self.sat[eng].get(sid, 0) < val:
                prev = need.get(sid)
                need[sid] = (sem, val if prev is None else max(val, prev[1]))
        for sid, (sem, val) in need.items():
            if eng == "pe" and sem is self.esem["pe"]:
                self.sat[eng][sid] = max(self.sat[eng].get(sid, 0), val)
                continue
            self.prog[eng].append(lambda q, s=sem, v=val: q.wait_ge(s, v))
            self.sat[eng][sid] = val

    def _record(self, tok, reads, writes):
        for b in reads:
            self.state[b]["r"].append(tok)
        for b in writes:
            self.state[b] = dict(w=tok, r=[])

    def op(self, eng, fn, reads=(), writes=()):
        toks = self._conflict_toks(reads, writes)
        self._emit_waits(eng, toks)
        sem = self.esem[eng]
        self.cnt[eng] += 1
        v = self.cnt[eng]
        self.prog[eng].append(lambda q, f=fn, s=sem: f(q).then_inc(s, 1))
        tok = (sem, v)
        self._record(tok, reads, writes)
        return tok

    def dma(self, q, stream, fn, reads=(), writes=()):
        toks = self._conflict_toks(reads, writes)
        self._emit_waits(q, toks)
        ds = self.dstream(stream)
        ds[1] += 16
        v = ds[1]
        self.prog[q].append(lambda qq, f=fn, s=ds[0]: f(qq).then_inc(s, 16))
        tok = (ds[0], v)
        self._record(tok, reads, writes)
        return tok

    def finalize_drains(self):
        for name, (sem, cnt) in self.dsems.items():
            if cnt:
                self.prog["sp"].append(lambda q, s=sem, v=cnt: q.wait_ge(s, v))

    def emit(self, block):
        block.tensor(lambda t: [f(t) for f in self.prog["pe"]])
        block.vector(lambda v: [f(v) for f in self.prog["dve"]])
        block.scalar(lambda s: [f(s) for f in self.prog["act"]])
        block.sync(lambda s: [f(s) for f in self.prog["sp"]])
        block.gpsimd(lambda g: [f(g) for f in self.prog["pool"]])


def build_program(nsub=4):
    from contextlib import ExitStack
    nc = bass.Bass("TRN2", target_bir_lowering=False)
    kb = Kb(nc, nsub)

    xin_d = kb.dram("x", [ROWS, D], F32, kind="ExternalInput")
    wx_d = kb.dram("wx", [128, HQ * 2 * KW1 * 128], F32, kind="ExternalInput")
    w2x_d = kb.dram("w2x", [128, NBLK * KW2 * 512], F32, kind="ExternalInput")
    a13_d = kb.dram("a13x", [128, 132], F32, kind="ExternalInput")
    a2_d = kb.dram("a2x", [128, HQ * 2], F32, kind="ExternalInput")
    tabC_d = kb.dram("tabC", [128, FC * 17 * 128], BF16, kind="ExternalInput")
    tabS_d = kb.dram("tabS", [128, FC * 16 * 128], BF16, kind="ExternalInput")
    tabD_d = kb.dram("tabD", [128, FC * DC], BF16, kind="ExternalInput")
    rowb_d = kb.dram("rowb", [128, 1], U32, kind="ExternalInput")
    idn_d = kb.dram("idn", [128, 128], BF16, kind="ExternalInput")
    out_d = kb.dram("out", [ROWS, D], F32, kind="ExternalOutput")
    xn2_d = {(s_, t_): kb.dram(f"xn2_{s_}_{t_}", [128, 8192], BF16, kind="Internal")
             for s_ in range(nsub) for t_ in range(4)}
    wxc_d = kb.dram("wxc", [128, HQ * 2 * KW1 * 128], BF16, kind="Internal")
    w2c_d = kb.dram("w2c", [128, NBLK * KW2 * 512], BF16, kind="Internal")

    with ExitStack() as ctx:
        kb.init_sems(ctx)
        e = ctx.enter_context

        aV = e(nc.sbuf_tensor("aV", [128, 34304], U8))[:]
        aX = e(nc.sbuf_tensor("aX", [128, 33792], U8))[:]
        aW = e(nc.sbuf_tensor("aW", [128, 65536], U8))[:]
        aP = e(nc.sbuf_tensor("aP", [128, 32768], U8))[:]
        aY = e(nc.sbuf_tensor("aY", [128, 32768], U8))[:]

        xnb = [kb.carve(f"xnb{t}", "aV", aV, t * 8192, [128, 4096], BF16) for t in range(4)]
        wqb = [kb.carve(f"wq{j}", "aV", aV, 20480 + j * 4608, [128, 2304], BF16) for j in range(3)]
        gated = [kb.carve(f"gated{g}", "aV", aV, 12288 + g * 1024, [128, 512], BF16) for g in range(8)]

        xnT = kb.carve("xnT", "aX", aX, 0, [128, 33 * 512], BF16)
        w8h = [kb.carve(f"w8h{j}", "aX", aX, j * 6144, [128, 6 * 512], BF16) for j in range(4)]
        scrs = [kb.carve(f"xcs{j}", "aP", aP, j * 8448, [128, 4224], BF16) for j in range(2)]

        xin = [kb.carve("xin0", "aP", aP, 0, [128, 4096], F32),
               kb.carve("xin1", "aP", aP, 16384, [128, 4096], F32)]
        dti = [kb.carve(f"dti{j}", "aW", aW, j * 4224, [128, 2112], BF16) for j in range(12)]
        dti2 = ([kb.carve(f"dti2_{j}", "aV", aV, 4224 + j * 4224, [128, 2112], BF16)
                 for j in range(3)]
                + [kb.carve("dti2_3", "aX", aX, 0, [128, 2112], BF16),
                   kb.carve("dti2_4", "aV", aV, 0, [128, 2112], BF16)])
        s_sl = [[kb.carve(f"s_{t}_{n}", "aW", aW, t * 16384 + n * 2048, [128, 512], F32)
                 for n in range(NBLK)] for t in range(4)]
        s_full = [kb.carve(f"sfull{t}", "aW", aW, t * 16384, [128, 4096], F32) for t in range(4)]

        Pb = kb.carve("P", "aP", aP, 0, [128, FC * 512], BF16)
        xepi = [kb.carve(f"xepi{j}", "aX", aX, 24576 + j * 2048, [128, 512], F32) for j in range(2)]

        tabC_s = [kb.carve(f"tabC{j}", "aY", aY, j * 4352, [128, 17 * 128], BF16) for j in range(2)]
        tabS_s = [kb.carve(f"tabS{j}", "aY", aY, 8704 + j * 4096, [128, 16 * 128], BF16) for j in range(2)]
        corr = [kb.carve(f"corr{t}", "aY", aY, t * 8192, [128, 4096], BF16) for t in range(4)]
        gfold = kb.carve("gfold", "aY", aY, 0, [128, GF * 512], BF16)
        gl2 = kb.carve("gl2", "aY", aY, GF * 1024, [128, 512], BF16)
        gout = [kb.carve("gout0", "aY", aY, 24576, [128, 4096], BF16),
                kb.carve("gout1", "aV", aV, 0, [128, 4096], BF16)]

        xftb = [kb.carve(f"xftb{j}", "aY", aY, 24576 + j * 2048, [128, 1024], BF16) for j in range(4)]
        xfT = kb.sb("xfT", [128, 8 * 512], BF16)
        xl13 = kb.sb("xl13", [128, 512], BF16)
        tmpA = kb.carve("gtmp0", "aV", aV, 0, [128, 512], F32)
        tmpB = kb.carve("gtmp1", "aV", aV, 2048, [128, 512], F32)
        foldt = kb.carve("foldt", "aX", aX, 28672, [128, 512], F32)
        aXj = kb.carve("aXj", "aX", aX, 30720, [128, 512], F32)
        gtmp = [foldt, aXj]
        gtmp_names = ["foldt", "aXj"]
        idb = kb.sb("idb", [128, 128], BF16)
        a13c = kb.sb("a13c", [128, 132], BF16)
        a2c = kb.sb("a2c", [128, HQ * 2], BF16)
        rowb_s = kb.sb("rowb_s", [128, 1], U32)
        max8b = kb.sb("max8b", [128, 8], BF16)
        max8f = kb.sb("max8f", [128, 8], F32)
        d8 = kb.sb("d8", [128, 8], U32)
        idxall = kb.sb("idxall", [128, 32], U32)
        idxa = [kb.reg(f"idxa{t}", idxall[:, t * 8:(t + 1) * 8], "idxall", t * 32, (t + 1) * 32)
                for t in range(4)]
        e8 = kb.sb("e8", [128, 8], F32)
        w8all = kb.sb("w8all", [128, 32], F32)
        w8a = [kb.reg(f"w8a{t}", w8all[:, t * 8:(t + 1) * 8], "w8all", t * 32, (t + 1) * 32)
               for t in range(4)]
        colsT = kb.sb("colsT", [128, 16], F32)
        colsAP = colsT
        def _col(i, name):
            return kb.reg(name, colsAP[:, i:i + 1], "colsT", i * 4, (i + 1) * 4)
        sm1 = _col(0, "sm1")
        sq1 = _col(1, "sq1")
        mcol = _col(2, "mcol")
        mnegl = _col(3, "mnegl")
        vcol = _col(4, "vcol")
        rstd = _col(5, "rstd")
        mbias = _col(6, "mbias")
        mneg = _col(7, "mneg")
        ssum = _col(8, "ssum")
        epsc = _col(9, "epsc")
        srec = _col(10, "srec")
        colsB = kb.sb("colsB", [128, 8], F32)
        colsC = kb.sb("colsC", [128, 8], F32)
        rstd_t = [kb.reg(f"rstd{t}", colsB[:, t:t + 1], "colsB", t * 4, (t + 1) * 4) for t in range(4)]
        mcol_t = [kb.reg(f"mcol{t}", colsB[:, 4 + t:5 + t], "colsB", 16 + t * 4, 20 + t * 4) for t in range(4)]

        pb = [kb.reg(f"pb{i}", e(nc.psum_tensor(f"pb{i}", [128, 512], F32))[:], f"pb{i}", 0, 2048)
              for i in range(4)]
        pbf = kb.reg("pbf", e(nc.psum_tensor("pbf", [128, 512], F32))[:], "pbf", 0, 2048)
        pt2 = kb.reg("pt2", e(nc.psum_tensor("pt2", [2, 512], F32))[:], "pt2", 0, 2048)
        ptrf = [e(nc.psum_tensor(f"ptr{i}", [128, 512], F32))[:] for i in range(2)]
        ptr = [kb.reg(f"ptr{i}", ptrf[i].bitcast(BF16), f"ptr{i}", 0, 2048) for i in range(2)]

        kb.op("dve", lambda q: q.memset(xl13, 0.0), writes=["xl13"])
        kb.op("dve", lambda q: q.memset(gl2, 0.0), writes=["gl2"])
        kb.op("dve", lambda q: q.memset(epsc, EPS), writes=["epsc"])
        kb.dma("sp", "ini_a", lambda q: q.dma_start(out=rowb_s, in_=rowb_d), writes=["rowb_s"])
        kb.dma("sp", "ini_b", lambda q: q.dma_start(out=idb, in_=idn_d), writes=["idb"])
        kb.dma("pool", "ini_c", lambda q: q.dma_start(out=a13c, in_=a13_d), writes=["a13c"])
        kb.dma("pool", "ini_d", lambda q: q.dma_start(out=a2c, in_=a2_d), writes=["a2c"])
        CPW = 4 * 2304
        cast_q = []
        for j in range(22):
            sl = slice(j * CPW, (j + 1) * CPW)
            kb.dram_region(f"wxcp{j}", "wxc", sl.start * 2, sl.stop * 2)
            cast_q.append((lambda q, sl=sl: q.dma_start(out=wxc_d[:, sl], in_=wx_d[:, sl]),
                           f"wxcp{j}"))
        for i in range(NBLK):
            sl = slice(i * KW2 * 512, (i + 1) * KW2 * 512)
            kb.dram_region(f"w2cp{i}", "w2c", sl.start * 2, sl.stop * 2)
            cast_q.append((lambda q, sl=sl: q.dma_start(out=w2c_d[:, sl], in_=w2x_d[:, sl]),
                           f"w2cp{i}"))
        _ci = [0]

        def drip_cast(k):
            while k > 0 and _ci[0] < len(cast_q):
                fn, wname = cast_q[_ci[0]]
                _ci[0] += 1
                k -= 1
                kb.dma("pool", "cast", fn, writes=[wname])

        xnT3 = xnT.rearrange("p (k t) -> p k t", t=512)
        xfT3 = xfT.rearrange("p (k t) -> p k t", t=512)
        Pb3 = Pb.rearrange("p (k t) -> p k t", t=512)
        gf3 = gfold.rearrange("p (k t) -> p k t", t=512)
        wq3 = [w.rearrange("p (a k m) -> p a k m", a=2, m=128) for w in wqb]
        w8h3 = [w.rearrange("p (k t) -> p k t", t=512) for w in w8h]

        stats_done = [[False] * NTC for _ in range(nsub)]
        loads_done = [[False] * NTC for _ in range(nsub)]
        cores_done = [[False] * NTC for _ in range(nsub)]

        def load_task(s_, t_):
            base_ = s_ * SUB
            xslot = f"xin{t_ % 2}"
            kb.dma("pool", xslot, lambda q, r=base_ + t_ * 128, j=t_ % 2: q.dma_start(
                out=xin[j], in_=xin_d[r:r + 128, :]), writes=[xslot])

        def stat_core(s_, t_, junk=None, junk_name=None):
            if junk is None:
                junk, junk_name = tmpA, "gtmp0"
            xslot = f"xin{t_ % 2}"
            xap = xin[t_ % 2]
            kb.op("dve", lambda q, a=xap: q.reduce_sum(sm1, a, axis=AX.X), reads=[xslot], writes=["sm1"])
            for j8 in range(8):
                kb.op("act", lambda q, a=xap, jj=j8, jk=junk: q.activation(
                    jk, a[:, jj * 512:(jj + 1) * 512],
                    AF.Square, accum_out=colsC[:, jj:jj + 1]),
                    reads=[xslot], writes=[junk_name, "colsC"])
            kb.op("dve", lambda q: q.reduce_sum(sq1, colsC, axis=AX.X), reads=["colsC"], writes=["sq1"])
            kb.op("dve", lambda q, tt=t_: q.tensor_scalar(mcol_t[tt], sm1, 1.0 / D, None, op0=ALU.mult),
                  reads=["sm1"], writes=[f"mcol{t_}"])
            kb.op("dve", lambda q, tt=t_: q.tensor_scalar(mnegl, mcol_t[tt], -1.0, None, op0=ALU.mult),
                  reads=[f"mcol{t_}"], writes=["mnegl"])
            kb.op("dve", lambda q: q.tensor_scalar(vcol, sq1, 1.0 / D, None, op0=ALU.mult),
                  reads=["sq1"], writes=["vcol"])
            kb.op("dve", lambda q, tt=t_: q.scalar_tensor_tensor(
                vcol, mcol_t[tt], mnegl, vcol, op0=ALU.mult, op1=ALU.add),
                reads=[f"mcol{t_}", "mnegl", "vcol"], writes=["vcol"])
            kb.op("act", lambda q: q.activation(vcol, vcol, AF.Ln, bias=epsc), reads=["vcol", "epsc"], writes=["vcol"])
            kb.op("act", lambda q, tt=t_: q.activation(rstd_t[tt], vcol, AF.Exp, scale=-0.5),
                  reads=["vcol"], writes=[f"rstd{t_}"])

        def norm_part(s_, t_):
            xslot = f"xin{t_ % 2}"
            xap = xin[t_ % 2]
            for sl4 in range(4):
                kb.op("dve", lambda q, a=xap, o=xnb[t_], tt=t_, ss=sl4: q.tensor_scalar(
                    o[:, ss * 1024:(ss + 1) * 1024], a[:, ss * 1024:(ss + 1) * 1024],
                    mcol_t[tt], rstd_t[tt], op0=ALU.subtract, op1=ALU.mult),
                    reads=[xslot, f"mcol{t_}", f"rstd{t_}"], writes=[f"xnb{t_}"])
            xfb = xftb[t_]
            xfn = f"xftb{t_}"
            kb.op("dve", lambda q, o=xnb[t_], w=xfb: q.tensor_add(w, o[:, 0:1024], o[:, 1024:2048]),
                  reads=[f"xnb{t_}"], writes=[xfn])
            kb.op("dve", lambda q, o=xnb[t_], w=xfb: q.tensor_add(w, w, o[:, 2048:3072]),
                  reads=[f"xnb{t_}", xfn], writes=[xfn])
            kb.op("dve", lambda q, o=xnb[t_], w=xfb: q.tensor_add(w, w, o[:, 3072:4096]),
                  reads=[f"xnb{t_}", xfn], writes=[xfn])
            kb.dma("pool", f"x2w{s_}_{t_}", lambda q, o=xnb[t_], d=xn2_d[(s_, t_)]: q.dma_start(
                out=d[:, 0:4096], in_=o), reads=[f"xnb{t_}"], writes=[f"xn2_{s_}_{t_}"])
            kb.dma("pool", f"x2w{s_}_{t_}", lambda q, o=xnb[t_], d=xn2_d[(s_, t_)]: q.dma_start(
                out=d[:, 4096:8192], in_=o), reads=[f"xnb{t_}"], writes=[f"xn2_{s_}_{t_}"])

        def stat_task(s_, t_):
            stats_done[s_][t_] = True
            load_task(s_, t_)
            stat_core(s_, t_, *((aXj, "aXj") if t_ == 1 else (None, None)))
            norm_part(s_, t_)

        def t_phase(t):
            kb.op("dve", lambda q, tt=t: q.max(out=max8b, in_=corr[tt]), reads=[f"corr{t}"], writes=["max8b"])
            kb.op("dve", lambda q, tt=t: q.max_index(out=d8, in_max=max8b, in_values=corr[tt]),
                  reads=[f"corr{t}", "max8b"], writes=["d8"])
            kb.op("dve", lambda q: q.tensor_copy(out=max8f, in_=max8b), reads=["max8b"], writes=["max8f"])
            kb.op("dve", lambda q: q.tensor_scalar(mneg, max8f[:, 0:1], -1.0, None, op0=ALU.mult),
                  reads=["max8f"], writes=["mneg"])
            kb.op("act", lambda q: q.activation(e8, max8f, AF.Exp, bias=mneg),
                  reads=["max8f", "mneg"], writes=["e8"])
            kb.op("dve", lambda q: q.reduce_sum(ssum, e8, axis=AX.X), reads=["e8"], writes=["ssum"])
            kb.op("dve", lambda q: q.reciprocal(srec, ssum), reads=["ssum"], writes=["srec"])
            kb.op("dve", lambda q, tt=t: q.tensor_scalar(w8a[tt], e8, srec, None, op0=ALU.mult),
                  reads=["e8", "srec"], writes=[f"w8a{t}"])
            kb.op("dve", lambda q, tt=t: q.tensor_tensor(
                idxa[tt], d8, rowb_s[:, 0:1].to_broadcast([128, 8]), op=ALU.add),
                reads=["d8", "rowb_s"], writes=[f"idxa{t}"])

        for s in range(nsub):
            base = s * SUB
            for t in range(NTC):
                if not stats_done[s][t]:
                    if not loads_done[s][t]:
                        load_task(s, t)
                    if not cores_done[s][t]:
                        stat_core(s, t)
                    norm_part(s, t)
                    stats_done[s][t] = True
                scr, sname2 = scrs[t % 2], f"xcs{t % 2}"
                kb.op("dve", lambda q, a=xnb[t], o=scr: q.tensor_add(
                    o[:, 0:2047], a[:, 1:2048], a[:, 4095:2048:-1]),
                    reads=[f"xnb{t}"], writes=[sname2])
                kb.op("dve", lambda q, a=xnb[t], o=scr: q.tensor_copy(
                    out=o[:, 2047:2048], in_=a[:, 0:1]), reads=[f"xnb{t}"], writes=[sname2])
                kb.op("dve", lambda q, a=xnb[t], o=scr: q.tensor_copy(
                    out=o[:, 2048:2049], in_=a[:, 2048:2049]), reads=[f"xnb{t}"], writes=[sname2])
                kb.op("dve", lambda q, o=scr: q.memset(o[:, 2049:2176], 0.0), writes=[sname2])
                kb.op("dve", lambda q, a=xnb[t], o=scr: q.tensor_tensor(
                    o[:, 2176:4223], a[:, 1:2048], a[:, 4095:2048:-1], op=ALU.subtract),
                    reads=[f"xnb{t}"], writes=[sname2])
                kb.op("dve", lambda q, o=scr: q.memset(o[:, 4223:4224], 0.0), writes=[sname2])
                for gi, (boff, bln) in enumerate([(0, 8), (8, 8), (16, 8), (24, 8), (32, 1)]):
                    pslot, pap = f"ptr{gi % 2}", ptr[gi % 2]
                    for j in range(bln):
                        fcn = boff + j
                        kb.op("pe", lambda q, src=scr, f=fcn, o=pap, jj=j: q.transpose(
                            o[:, jj * 128:(jj + 1) * 128], src[:, f * 128:(f + 1) * 128], idb),
                            reads=[sname2, "idb"], writes=[pslot])
                    dst = xnT3[:, boff:boff + bln, t * 128:(t + 1) * 128]
                    kb.op("act", lambda q, o=dst, i=pap, ln=bln: q.copy(
                        o, i[:, 0:ln * 128].rearrange("p (j m) -> p j m", m=128)),
                        reads=[pslot], writes=["xnT"])
                for j in range(8):
                    kb.op("pe", lambda q, f=j, o=ptr[0], jj=j, w=xftb[t]: q.transpose(
                        o[:, jj * 128:(jj + 1) * 128], w[:, f * 128:(f + 1) * 128], idb),
                        reads=[f"xftb{t}", "idb"], writes=["ptr0"])
                dst = xfT3[:, :, t * 128:(t + 1) * 128]
                kb.op("act", lambda q, o=dst, i=ptr[0]: q.copy(o, i.rearrange("p (j m) -> p j m", m=128)),
                      reads=["ptr0"], writes=["xfT"])

            for j in range(12):
                kb.dma("act", "dgA", lambda q, k=j: q.dma_start(
                    out=dti[k], in_=tabD_d[:, k * DC:(k + 1) * DC]), writes=[f"dti{j}"])
            drip_cast(6)
            for j in range(3):
                kb.dma("pool", "dgB", lambda q, k=j: q.dma_start(
                    out=dti2[k], in_=tabD_d[:, (12 + k) * DC:(13 + k) * DC]), writes=[f"dti2_{j}"])
            for kc in range(33):
                kb.op("pe", lambda q, k=kc: q.matmul(pbf[0:4, :], lhsT=a13c[:, k * 4:k * 4 + 4],
                                                     rhs=xnT[:, k * 512:(k + 1) * 512],
                                                     start=(k == 0), stop=(k == 32)),
                      reads=["a13c", "xnT"], writes=["pbf"])
            kb.op("act", lambda q: q.mul(xl13[0:4, :], pbf[0:4, :], 0.5), reads=["pbf"], writes=["xl13"])

            kb.dma("sp", "tabC0", lambda q: q.dma_start(out=tabC_s[0], in_=tabC_d[:, 0:2176]), writes=["tabC0"])
            kb.dma("sp", "tabS0", lambda q: q.dma_start(out=tabS_s[0], in_=tabS_d[:, 0:2048]), writes=["tabS0"])
            for fc in range(FC):
                if fc + 1 < FC:
                    j = (fc + 1) % 2
                    kb.dma("sp", f"tabC{j}", lambda q, f=fc + 1, jj=j: q.dma_start(
                        out=tabC_s[jj], in_=tabC_d[:, f * 2176:(f + 1) * 2176]), writes=[f"tabC{j}"])
                    kb.dma("sp", f"tabS{j}", lambda q, f=fc + 1, jj=j: q.dma_start(
                        out=tabS_s[jj], in_=tabS_d[:, f * 2048:(f + 1) * 2048]), writes=[f"tabS{j}"])
                cslot, sslot = f"tabC{fc % 2}", f"tabS{fc % 2}"
                cap, sap = tabC_s[fc % 2], tabS_s[fc % 2]
                rb, ib = f"pb{fc % 2}", f"pb{2 + fc % 2}"
                rap, iap = pb[fc % 2], pb[2 + fc % 2]
                for kc in range(17):
                    kb.op("pe", lambda q, k=kc, c=cap, o=rap: q.matmul(
                        o, lhsT=c[:, k * 128:(k + 1) * 128], rhs=xnT[:, k * 512:(k + 1) * 512],
                        start=(k == 0), stop=(k == 16)), reads=[cslot, "xnT"], writes=[rb])
                for kc in range(16):
                    kb.op("pe", lambda q, k=kc, c=sap, o=iap: q.matmul(
                        o, lhsT=c[:, k * 128:(k + 1) * 128], rhs=xnT[:, (17 + k) * 512:(18 + k) * 512],
                        start=(k == 0), stop=(k == 15)), reads=[sslot, "xnT"], writes=[ib])
                kb.op("act", lambda q, i=rap: q.activation(tmpA, i, AF.Square), reads=[rb], writes=["gtmp0"])
                kb.op("act", lambda q, i=iap: q.activation(tmpB, i, AF.Square), reads=[ib], writes=["gtmp1"])
                kb.op("dve", lambda q, f=fc: q.tensor_add(Pb[:, f * 512:(f + 1) * 512], tmpA, tmpB),
                      reads=["gtmp0", "gtmp1"], writes=["P"])
            for j in range(3, 5):
                kb.dma("pool", "dgB", lambda q, k=j: q.dma_start(
                    out=dti2[k], in_=tabD_d[:, (12 + k) * DC:(13 + k) * DC]), writes=[f"dti2_{j}"])

            for j0 in range(2):
                kb.dma("sp", f"wq{j0}", lambda q, i=j0: q.dma_start(
                    out=wqb[i], in_=wxc_d[:, i * 2304:(i + 1) * 2304]),
                    reads=["wxcp0"], writes=[f"wq{j0}"])
            for t in range(NTC):
                for n in range(NBC):
                    bank = (t * NBC + n) % 4
                    bap, bname = pb[bank], f"pb{bank}"
                    nw = 512 if n < 4 else 1
                    for kc in range(FC):
                        dd = dti[kc] if kc < 12 else dti2[kc - 12]
                        dname = f"dti{kc}" if kc < 12 else f"dti2_{kc - 12}"
                        kb.op("pe", lambda q, k=kc, tt=t, nn=n, o=bap, w=nw, d_=dd: q.matmul(
                            o[:, 0:w], lhsT=Pb3[:, k, tt * 128:(tt + 1) * 128],
                            rhs=d_[:, nn * 512:nn * 512 + w],
                            start=(k == 0), stop=(k == FC - 1)),
                            reads=["P", dname], writes=[bname])
                    kb.op("act", lambda q, tt=t, nn=n, i=bap, w=nw: q.copy(
                        corr[tt][:, nn * 512:nn * 512 + w], i[:, 0:w]),
                        reads=[bname], writes=[f"corr{t}"])
                kb.op("dve", lambda q, tt=t: q.tensor_copy(
                    out=corr[tt][:, 2049:4096], in_=corr[tt][:, 2047:0:-1]),
                    reads=[f"corr{t}"], writes=[f"corr{t}"])
                t_phase(t)
                drip_cast(1)

            def mac_task(t, j):
                gslot, gap = f"gout{(t * 8 + j) % 2}", gout[(t * 8 + j) % 2]
                kb.dma("pool", f"g{(t * 8 + j) % 2}", lambda q, jj=j, o=gap, d=xn2_d[(s, t)], tt=t:
                       q.indirect_dma_start(
                           out=o, out_offset=None, in_=d,
                           in_offset=IndirectOffsetOnAxis(idxa[tt][:, jj:jj + 1], axis=1)),
                       reads=[f"idxa{t}", f"xn2_{s}_{t}"], writes=[gslot])
                sname = f"sfull{t}"
                if j == 0:
                    kb.op("dve", lambda q, g=gap, tt=t: q.tensor_scalar(
                        s_full[tt], g, w8a[tt][:, 0:1], None, op0=ALU.mult),
                        reads=[gslot, f"w8a{t}"], writes=[sname])
                else:
                    kb.op("dve", lambda q, g=gap, jj=j, tt=t: q.scalar_tensor_tensor(
                        s_full[tt], g, w8a[tt][:, jj:jj + 1], s_full[tt],
                        op0=ALU.mult, op1=ALU.add),
                        reads=[gslot, f"w8a{t}", sname], writes=[sname])

            mac_tasks = [(t, j) for t in range(NTC) for j in range(8)]
            mac_i = 0

            def emit_t2_fold(cc):
                for g in range(4):
                    qi = cc * 4 + g
                    gs = 4 * (cc % 2) + g
                    kb.op("pe", lambda q, i=qi, gg=gs: q.matmul(
                        pt2[:], lhsT=a2c[:, i * 2:(i + 1) * 2], rhs=gated[gg],
                        start=(i == 0), stop=(i == HQ - 1)),
                        reads=["a2c", f"gated{gs}"], writes=["pt2"])
                g0, g1, g2, g3 = (4 * (cc % 2) + g for g in range(4))
                kb.op("dve", lambda q, a=gated[g0], b=gated[g1]: q.tensor_add(foldt, a, b),
                      reads=[f"gated{g0}", f"gated{g1}"], writes=["foldt"])
                kb.op("dve", lambda q, a=gated[g2]: q.tensor_add(foldt, foldt, a),
                      reads=[f"gated{g2}", "foldt"], writes=["foldt"])
                kb.op("dve", lambda q, a=gated[g3], cc=cc: q.tensor_add(
                    gfold[:, cc * 512:(cc + 1) * 512], foldt, a),
                    reads=[f"gated{g3}", "foldt"], writes=["gfold"])

            for c in range(GF):
                for g in range(4):
                    qi = c * 4 + g
                    slot = qi % 3
                    gs = 4 * (c % 2) + g
                    if qi + 2 < HQ:
                        ns = (qi + 2) % 3
                        kb.dma("sp", f"wq{ns}", lambda q, i=qi + 2, jj=ns: q.dma_start(
                            out=wqb[jj], in_=wxc_d[:, i * 2304:(i + 1) * 2304]),
                            reads=[f"wxcp{(qi + 2) // 4}"], writes=[f"wq{ns}"])
                    h1n, h3n = f"pb{2 * (qi % 2)}", f"pb{2 * (qi % 2) + 1}"
                    h1ap, h3ap = pb[2 * (qi % 2)], pb[2 * (qi % 2) + 1]
                    wslot, wap = f"wq{slot}", wq3[slot]
                    for kc in range(KW1):
                        rhs = xfT[:, kc * 512:(kc + 1) * 512] if kc < 8 else xl13
                        rname = "xfT" if kc < 8 else "xl13"
                        kb.op("pe", lambda q, k=kc, w=wap, o=h1ap, r=rhs: q.matmul(
                            o, lhsT=w[:, 0, k, :], rhs=r, start=(k == 0), stop=(k == KW1 - 1)),
                            reads=[wslot, rname], writes=[h1n])
                    for kc in range(KW1):
                        rhs = xfT[:, kc * 512:(kc + 1) * 512] if kc < 8 else xl13
                        rname = "xfT" if kc < 8 else "xl13"
                        kb.op("pe", lambda q, k=kc, w=wap, o=h3ap, r=rhs: q.matmul(
                            o, lhsT=w[:, 1, k, :], rhs=r, start=(k == 0), stop=(k == KW1 - 1)),
                            reads=[wslot, rname], writes=[h3n])
                    if g == 3 and c >= 1:
                        emit_t2_fold(c - 1)
                    kb.op("act", lambda q, i=h1ap, o=gated[gs]: q.activation(o, i, AF.Silu),
                          reads=[h1n], writes=[f"gated{gs}"])
                    kb.op("dve", lambda q, gg=gs, hp=h3ap: q.tensor_mul(gated[gg], gated[gg], hp),
                          reads=[f"gated{gs}", h3n], writes=[f"gated{gs}"])
                    if c >= 2 and g % 2 == 1 and mac_i < len(mac_tasks):
                        mac_task(*mac_tasks[mac_i])
                        mac_i += 1
                    if g == 3:
                        drip_cast(2)
                    if g == 3 and c in (19, 21) and s + 1 < nsub and not stats_done[s + 1][0 if c == 19 else 1]:
                        stat_task(s + 1, 0 if c == 19 else 1)
            emit_t2_fold(GF - 1)
            while mac_i < len(mac_tasks):
                mac_task(*mac_tasks[mac_i])
                mac_i += 1
            kb.op("act", lambda q: q.mul(gl2[0:2, :], pt2[:], 0.5), reads=["pt2"], writes=["gl2"])

            def w2piece(n, p):
                off = n * KW2 * 512 + 6 * p * 512
                ln = (6 if p < 3 else 5) * 512
                return off, ln
            pieces = [(n, p) for n in range(NBLK) for p in range(4)]
            for pre in range(3):
                n2, p2 = pieces[pre]
                off, ln = w2piece(n2, p2)
                kb.dma("act", f"w8h{pre}", lambda q, o=off, l=ln, jj=pre: q.dma_start(
                    out=w8h[jj][:, 0:l], in_=w2c_d[:, o:o + l]), reads=[f"w2cp{n2}"], writes=[f"w8h{pre}"])
            ep = 0
            for pi, (n, p) in enumerate(pieces):
                slot = pi % 4
                if pi + 3 < len(pieces):
                    n2, p2 = pieces[pi + 3]
                    off, ln = w2piece(n2, p2)
                    ns = (pi + 3) % 4
                    kb.dma("act", f"w8h{ns}", lambda q, o=off, l=ln, jj=ns: q.dma_start(
                        out=w8h[jj][:, 0:l], in_=w2c_d[:, o:o + l]), reads=[f"w2cp{n2}"], writes=[f"w8h{ns}"])
                kcs = range(6 * p, min(6 * p + 6, KW2))
                for t in range(NTC):
                    bank = (n * 4 + t) % 5
                    bap, bname = (pb + [pbf])[bank], (["pb0", "pb1", "pb2", "pb3", "pbf"])[bank]
                    for kc in kcs:
                        if kc < GF:
                            lhs, lname = gf3[:, kc, t * 128:(t + 1) * 128], "gfold"
                        else:
                            lhs, lname = gl2[:, t * 128:(t + 1) * 128], "gl2"
                        ko = kc - 6 * p
                        kb.op("pe", lambda q, k=ko, o=bap, l=lhs, w=w8h3[slot], st=(kc == 0), sp=(kc == KW2 - 1): q.matmul(
                            o, lhsT=l, rhs=w[:, k, :], start=st, stop=sp),
                            reads=[lname, f"w8h{slot}"], writes=[bname])
                if p == 3:
                    if s + 1 < nsub:
                        if n == 0 and not loads_done[s + 1][2]:
                            loads_done[s + 1][2] = True
                            load_task(s + 1, 2)
                        if n == 1 and not cores_done[s + 1][2]:
                            cores_done[s + 1][2] = True
                            stat_core(s + 1, 2, aXj, "aXj")
                        if n == 2 and not stats_done[s + 1][2]:
                            stats_done[s + 1][2] = True
                            norm_part(s + 1, 2)
                            if not loads_done[s + 1][3]:
                                loads_done[s + 1][3] = True
                                load_task(s + 1, 3)
                        if n == 3 and not cores_done[s + 1][3]:
                            cores_done[s + 1][3] = True
                            stat_core(s + 1, 3, aXj, "aXj")
                        if n == 4 and not stats_done[s + 1][3]:
                            stats_done[s + 1][3] = True
                            norm_part(s + 1, 3)
                    for t in range(NTC):
                        bank = (n * 4 + t) % 5
                        bap, bname = (pb + [pbf])[bank], (["pb0", "pb1", "pb2", "pb3", "pbf"])[bank]
                        gsl, gapp = gtmp_names[ep % 2], gtmp[ep % 2]
                        kb.op("act", lambda q, i=bap, o=gapp: q.activation(o, i, AF.Silu),
                              reads=[bname], writes=[gsl])
                        xs, xa = f"xepi{ep % 2}", xepi[ep % 2]
                        r0 = base + t * 128
                        kb.dma("pool", xs, lambda q, r=r0, nn=n, o=xa: q.dma_start(
                            out=o, in_=xin_d[r:r + 128, nn * 512:(nn + 1) * 512]), writes=[xs])
                        sname = f"s_{t}_{n}"
                        kb.op("dve", lambda q, tt=t, nn=n, g=gapp: q.tensor_mul(s_sl[tt][nn], s_sl[tt][nn], g),
                              reads=[sname, gsl], writes=[sname])
                        kb.op("dve", lambda q, tt=t, nn=n, x=xa: q.tensor_add(s_sl[tt][nn], s_sl[tt][nn], x),
                              reads=[sname, xs], writes=[sname])
                        kb.dma("pool", "outw", lambda q, r=r0, tt=t, nn=n: q.dma_start(
                            out=out_d[r:r + 128, nn * 512:(nn + 1) * 512], in_=s_sl[tt][nn]),
                            reads=[sname])
                        ep += 1

        kb.finalize_drains()
        block = e(nc.Block())
        kb.emit(block)
    return nc


from concourse.bass_utils import run_bass_kernel_spmd

_NCORES = 8


def kernel(L0_attn_x, L0_ffn_w1q, L0_ffn_a1, L0_ffn_b1, L0_ffn_w3q, L0_ffn_a3,
           L0_ffn_b3, L0_ffn_w2q, L0_ffn_a2, L0_ffn_b2):
    x = np.ascontiguousarray(np.asarray(L0_attn_x, dtype=np.float32).reshape(16384, D))
    hp = host_prepare(L0_ffn_w1q, L0_ffn_a1, L0_ffn_b1, L0_ffn_w3q, L0_ffn_a3,
                      L0_ffn_b3, L0_ffn_w2q, L0_ffn_a2, L0_ffn_b2)
    shards = np.split(x, _NCORES, axis=0)
    in_maps = [dict(x=shards[i], **hp) for i in range(_NCORES)]
    nc = build_program(nsub=4)
    res = run_bass_kernel_spmd(nc, in_maps, core_ids=list(range(_NCORES)))
    out = np.concatenate([res.results[i]["out"] for i in range(_NCORES)], axis=0)
    return out.reshape(2, 8192, D)


# version 17
# speedup vs baseline: 1.1300x; 1.0771x over previous
import numpy as np
import ml_dtypes
import concourse.bass as bass
import concourse.mybir as mybir
from concourse.bass import IndirectOffsetOnAxis

F32 = mybir.dt.float32
BF16 = mybir.dt.bfloat16
U32 = mybir.dt.uint32
U8 = mybir.dt.uint8
AF = mybir.ActivationFunctionType
ALU = mybir.AluOpType
AX = mybir.AxisListType

ROWS = 2048
SUB = 512
NTC = SUB // 128
D = 4096
FC = 17
H = 11008
HPAD = 11264
HQ = 88
GF = 22
KW1 = 9
KW2 = 23
NBLK = 8
DC = 2112
NBC = 5
EPS = 1e-5


def host_prepare(w1q, a1, b1, w3q, a3, b3, w2q, a2, b2):
    f32 = np.float32
    bf16 = ml_dtypes.bfloat16
    order = [22 * g + c for c in range(22) for g in range(4)]

    def padded_cols(mat_t):
        r = mat_t.shape[0]
        out = np.zeros((r, HPAD), dtype=mat_t.dtype)
        for g in range(4):
            out[:, g * 2816: g * 2816 + 2752] = mat_t[:, g * 2752:(g + 1) * 2752]
        return out

    w1k = np.zeros((KW1 * 128, HPAD), dtype=f32)
    w1k[0:1024] = padded_cols(np.ascontiguousarray(w1q.T).astype(f32, copy=False))
    w1k[1024:1026] = padded_cols(np.ascontiguousarray(b1.T).astype(f32, copy=False))
    w3k = np.zeros((KW1 * 128, HPAD), dtype=f32)
    w3k[0:1024] = padded_cols(np.ascontiguousarray(w3q.T).astype(f32, copy=False))
    w3k[1026:1028] = padded_cols(np.ascontiguousarray(b3.T).astype(f32, copy=False))
    w1r = w1k.reshape(KW1, 128, HQ, 128)[:, :, order, :]
    w3r = w3k.reshape(KW1, 128, HQ, 128)[:, :, order, :]
    wx = np.empty((128, HQ, 2, KW1, 128), dtype=f32)
    wx[:, :, 0] = np.transpose(w1r, (1, 2, 0, 3))
    wx[:, :, 1] = np.transpose(w3r, (1, 2, 0, 3))
    wx = np.ascontiguousarray(wx.reshape(128, HQ * 2 * KW1 * 128))

    w2rows = np.zeros((KW2 * 128, D), dtype=f32)
    w2rows[0:2752] = np.ascontiguousarray(w2q.T).astype(f32, copy=False)
    w2rows[2816:2818] = np.ascontiguousarray(b2.T).astype(f32, copy=False)
    w2r = w2rows.reshape(KW2, 128, NBLK, 512)
    w2x = np.ascontiguousarray(np.transpose(w2r, (1, 2, 0, 3)).reshape(128, NBLK * KW2 * 512))

    a13 = np.concatenate([np.ascontiguousarray(a1.T), np.ascontiguousarray(a3.T)], axis=1).astype(f32, copy=False)
    jj = np.arange(2047)
    Ac = np.zeros((2176, 4), dtype=f32)
    As = np.zeros((2048, 4), dtype=f32)
    Ac[jj] = (a13[jj + 1] + a13[4095 - jj]) * 0.5
    Ac[2047] = a13[0]
    Ac[2048] = a13[2048]
    As[jj] = (a13[jj + 1] - a13[4095 - jj]) * 0.5
    acs = np.concatenate([Ac, As], axis=0)
    a13x = np.ascontiguousarray(acs.reshape(33, 128, 4).transpose(1, 0, 2).reshape(128, 132))

    a2p = np.ascontiguousarray(padded_cols(np.ascontiguousarray(a2).astype(f32, copy=False)).T)
    a2x = np.ascontiguousarray(a2p.reshape(HQ, 128, 2)[order].transpose(1, 0, 2).reshape(128, HQ * 2))

    k = np.arange(FC * 128, dtype=np.int64)
    msk = (k <= 2048).astype(np.float64)[None, :]
    ii = (jj + 1).astype(np.int64)
    angf = ((ii[:, None] * k[None, :]) % D).astype(np.float64) * (2.0 * np.pi / D)
    Crows = np.zeros((2176, FC * 128))
    Crows[jj] = np.cos(angf)
    Crows[2047] = 1.0
    Crows[2048] = np.where(k % 2 == 0, 1.0, -1.0)
    Crows *= msk
    Srows = np.zeros((2048, FC * 128))
    Srows[jj] = np.sin(angf)
    Srows *= msk

    tabC = np.ascontiguousarray(
        Crows.reshape(17, 128, FC, 128).transpose(1, 2, 0, 3).reshape(128, FC * 17 * 128)).astype(bf16)
    tabS = np.ascontiguousarray(
        Srows.reshape(16, 128, FC, 128).transpose(1, 2, 0, 3).reshape(128, FC * 16 * 128)).astype(bf16)

    wk = np.ones(FC * 128, dtype=np.float64)
    wk[1:2048] = 2.0
    wk[2049:] = 0.0
    dd = np.arange(DC, dtype=np.int64)
    angd = ((k[:, None] * dd[None, :]) % D).astype(np.float64) * (2.0 * np.pi / D)
    Dt = np.cos(angd) * (wk[:, None] / D)
    Dt[:, 2049:] = 0.0
    tabD = np.ascontiguousarray(Dt.reshape(FC, 128, DC).transpose(1, 0, 2).reshape(128, FC * DC)).astype(bf16)

    rowb = (np.arange(128, dtype=np.uint32) * 8192).reshape(128, 1).copy()
    idn = np.eye(128, dtype=np.float32).astype(bf16)
    return dict(wx=wx, w2x=w2x, a13x=a13x, a2x=a2x,
                tabC=tabC, tabS=tabS, tabD=tabD, rowb=rowb, idn=idn)


class Kb:
    def __init__(self, nc, nsub=4):
        self.nc = nc
        self.nsub = nsub
        self.prog = {e: [] for e in ("pe", "dve", "act", "sp", "pool")}
        self.cnt = {e: 0 for e in self.prog}
        self.sat = {e: {} for e in self.prog}
        self.esem = {}
        self.bufs = {}
        self.state = {}
        self.dsems = {}
        self.semstream = {}
        self._ctx = None

    def init_sems(self, ctx):
        self._ctx = ctx
        for e in self.prog:
            self.esem[e] = ctx.enter_context(self.nc.semaphore(f"sem_{e}"))

    def dstream(self, name):
        if name not in self.dsems:
            sem = self._ctx.enter_context(self.nc.semaphore(f"d_{name}"))
            self.dsems[name] = [sem, 0]
            self.semstream[id(sem)] = name
        return self.dsems[name]

    def reg(self, name, ap, arena, lo, hi):
        self.bufs[name] = (arena, lo, hi)
        self.state[name] = dict(w=None, r=[])
        return ap

    def sb(self, name, shape, dtype):
        t = self._ctx.enter_context(self.nc.sbuf_tensor(name, shape, dtype))
        nbytes = int(np.prod(shape[1:])) * mybir.dt.size(dtype)
        return self.reg(name, t[:], name, 0, nbytes)

    def carve(self, name, arena_name, arena_ap, lo, shape, dtype):
        nbytes = int(np.prod(shape[1:])) * mybir.dt.size(dtype)
        assert lo + nbytes <= arena_ap.shape[1], (name, lo, nbytes, arena_ap.shape)
        view = arena_ap[:, lo:lo + nbytes].bitcast(dtype)
        return self.reg(name, view, arena_name, lo, lo + nbytes)

    def dram(self, name, shape, dtype, kind="Internal"):
        t = self.nc.dram_tensor(name, shape, dtype, kind=kind)
        nbytes = int(np.prod(shape)) * mybir.dt.size(dtype)
        return self.reg(name, t.ap(), "dram_" + name, 0, nbytes)

    def dram_region(self, name, base, lo, hi):
        return self.reg(name, None, "dram_" + base, lo, hi)

    def _conflict_toks(self, reads, writes):
        toks = []
        seen = set()
        items = [(n, False) for n in reads] + [(n, True) for n in writes]
        for name, is_w in items:
            arena, lo, hi = self.bufs[name]
            for other, (ar2, lo2, hi2) in self.bufs.items():
                if ar2 != arena or not (lo < hi2 and lo2 < hi):
                    continue
                st = self.state[other]
                if st["w"] is not None and ("w", other) not in seen:
                    toks.append(st["w"])
                    seen.add(("w", other))
                if is_w:
                    for t in st["r"]:
                        if ("r", other, t) not in seen:
                            toks.append(t)
                            seen.add(("r", other, t))
        return toks

    def _emit_waits(self, eng, toks):
        need = {}
        for (sem, val) in toks:
            sid = id(sem)
            if sid in self.semstream:
                val = self.dsems[self.semstream[sid]][1]
            if self.sat[eng].get(sid, 0) < val:
                prev = need.get(sid)
                need[sid] = (sem, val if prev is None else max(val, prev[1]))
        for sid, (sem, val) in need.items():
            if eng == "pe" and sem is self.esem["pe"]:
                self.sat[eng][sid] = max(self.sat[eng].get(sid, 0), val)
                continue
            self.prog[eng].append(lambda q, s=sem, v=val: q.wait_ge(s, v))
            self.sat[eng][sid] = val

    def _record(self, tok, reads, writes):
        for b in reads:
            self.state[b]["r"].append(tok)
        for b in writes:
            self.state[b] = dict(w=tok, r=[])

    def op(self, eng, fn, reads=(), writes=()):
        toks = self._conflict_toks(reads, writes)
        self._emit_waits(eng, toks)
        sem = self.esem[eng]
        self.cnt[eng] += 1
        v = self.cnt[eng]
        self.prog[eng].append(lambda q, f=fn, s=sem: f(q).then_inc(s, 1))
        tok = (sem, v)
        self._record(tok, reads, writes)
        return tok

    def dma(self, q, stream, fn, reads=(), writes=()):
        toks = self._conflict_toks(reads, writes)
        self._emit_waits(q, toks)
        ds = self.dstream(stream)
        ds[1] += 16
        v = ds[1]
        self.prog[q].append(lambda qq, f=fn, s=ds[0]: f(qq).then_inc(s, 16))
        tok = (ds[0], v)
        self._record(tok, reads, writes)
        return tok

    def finalize_drains(self):
        for name, (sem, cnt) in self.dsems.items():
            if cnt:
                self.prog["sp"].append(lambda q, s=sem, v=cnt: q.wait_ge(s, v))

    def emit(self, block):
        block.tensor(lambda t: [f(t) for f in self.prog["pe"]])
        block.vector(lambda v: [f(v) for f in self.prog["dve"]])
        block.scalar(lambda s: [f(s) for f in self.prog["act"]])
        block.sync(lambda s: [f(s) for f in self.prog["sp"]])
        block.gpsimd(lambda g: [f(g) for f in self.prog["pool"]])


def build_program(nsub=4):
    from contextlib import ExitStack
    nc = bass.Bass("TRN2", target_bir_lowering=False)
    kb = Kb(nc, nsub)

    xin_d = kb.dram("x", [ROWS, D], F32, kind="ExternalInput")
    wx_d = kb.dram("wx", [128, HQ * 2 * KW1 * 128], F32, kind="ExternalInput")
    w2x_d = kb.dram("w2x", [128, NBLK * KW2 * 512], F32, kind="ExternalInput")
    a13_d = kb.dram("a13x", [128, 132], F32, kind="ExternalInput")
    a2_d = kb.dram("a2x", [128, HQ * 2], F32, kind="ExternalInput")
    tabC_d = kb.dram("tabC", [128, FC * 17 * 128], BF16, kind="ExternalInput")
    tabS_d = kb.dram("tabS", [128, FC * 16 * 128], BF16, kind="ExternalInput")
    tabD_d = kb.dram("tabD", [128, FC * DC], BF16, kind="ExternalInput")
    rowb_d = kb.dram("rowb", [128, 1], U32, kind="ExternalInput")
    idn_d = kb.dram("idn", [128, 128], BF16, kind="ExternalInput")
    out_d = kb.dram("out", [ROWS, D], F32, kind="ExternalOutput")
    xn2_d = {(s_, t_): kb.dram(f"xn2_{s_}_{t_}", [128, 8192], BF16, kind="Internal")
             for s_ in range(nsub) for t_ in range(4)}
    wxc_d = kb.dram("wxc", [128, HQ * 2 * KW1 * 128], BF16, kind="Internal")
    w2c_d = kb.dram("w2c", [128, NBLK * KW2 * 512], BF16, kind="Internal")

    with ExitStack() as ctx:
        kb.init_sems(ctx)
        e = ctx.enter_context

        aV = e(nc.sbuf_tensor("aV", [128, 34304], U8))[:]
        aX = e(nc.sbuf_tensor("aX", [128, 33792], U8))[:]
        aW = e(nc.sbuf_tensor("aW", [128, 65536], U8))[:]
        aP = e(nc.sbuf_tensor("aP", [128, 32768], U8))[:]
        aY = e(nc.sbuf_tensor("aY", [128, 32768], U8))[:]

        xnb = [kb.carve(f"xnb{t}", "aV", aV, t * 8192, [128, 4096], BF16) for t in range(4)]
        wqb = [kb.carve(f"wq{j}", "aV", aV, 20480 + j * 4608, [128, 2304], BF16) for j in range(3)]
        gated = [kb.carve(f"gated{g}", "aV", aV, 12288 + g * 1024, [128, 512], BF16) for g in range(8)]

        xnT = kb.carve("xnT", "aX", aX, 0, [128, 33 * 512], BF16)
        w8h = [kb.carve(f"w8h{j}", "aX", aX, j * 6144, [128, 6 * 512], BF16) for j in range(4)]
        scrs = [kb.carve(f"xcs{j}", "aP", aP, j * 8448, [128, 4224], BF16) for j in range(2)]

        xin = [kb.carve("xin0", "aP", aP, 0, [128, 4096], F32),
               kb.carve("xin1", "aP", aP, 16384, [128, 4096], F32)]
        dti = [kb.carve(f"dti{j}", "aW", aW, j * 4224, [128, 2112], BF16) for j in range(12)]
        dti2 = ([kb.carve(f"dti2_{j}", "aV", aV, 4224 + j * 4224, [128, 2112], BF16)
                 for j in range(3)]
                + [kb.carve("dti2_3", "aX", aX, 0, [128, 2112], BF16),
                   kb.carve("dti2_4", "aV", aV, 0, [128, 2112], BF16)])
        s_sl = [[kb.carve(f"s_{t}_{n}", "aW", aW, t * 16384 + n * 2048, [128, 512], F32)
                 for n in range(NBLK)] for t in range(4)]
        s_full = [kb.carve(f"sfull{t}", "aW", aW, t * 16384, [128, 4096], F32) for t in range(4)]

        Pb = kb.carve("P", "aP", aP, 0, [128, FC * 512], BF16)
        xepi = [kb.carve(f"xepi{j}", "aX", aX, 24576 + j * 2048, [128, 512], F32) for j in range(2)]

        tabC_s = [kb.carve(f"tabC{j}", "aY", aY, j * 4352, [128, 17 * 128], BF16) for j in range(2)]
        tabS_s = [kb.carve(f"tabS{j}", "aY", aY, 8704 + j * 4096, [128, 16 * 128], BF16) for j in range(2)]
        corr = [kb.carve(f"corr{t}", "aY", aY, t * 8192, [128, 4096], BF16) for t in range(4)]
        gfold = kb.carve("gfold", "aY", aY, 0, [128, GF * 512], BF16)
        gl2 = kb.carve("gl2", "aY", aY, GF * 1024, [128, 512], BF16)
        gout = [kb.carve("gout0", "aY", aY, 24576, [128, 4096], BF16),
                kb.carve("gout1", "aV", aV, 0, [128, 4096], BF16)]

        xftb = [kb.carve(f"xftb{j}", "aY", aY, 24576 + j * 2048, [128, 1024], BF16) for j in range(4)]
        xfT = kb.sb("xfT", [128, 8 * 512], BF16)
        xl13 = kb.sb("xl13", [128, 512], BF16)
        tmpA = kb.carve("gtmp0", "aV", aV, 0, [128, 512], F32)
        tmpB = kb.carve("gtmp1", "aV", aV, 2048, [128, 512], F32)
        foldt = kb.carve("foldt", "aX", aX, 28672, [128, 512], F32)
        aXj = kb.carve("aXj", "aX", aX, 30720, [128, 512], F32)
        gtmp = [foldt, aXj]
        gtmp_names = ["foldt", "aXj"]
        idb = kb.sb("idb", [128, 128], BF16)
        a13c = kb.sb("a13c", [128, 132], BF16)
        a2c = kb.sb("a2c", [128, HQ * 2], BF16)
        rowb_s = kb.sb("rowb_s", [128, 1], U32)
        max8b = kb.sb("max8b", [128, 8], BF16)
        max8f = kb.sb("max8f", [128, 8], F32)
        d8 = kb.sb("d8", [128, 8], U32)
        idxall = kb.sb("idxall", [128, 32], U32)
        idxa = [kb.reg(f"idxa{t}", idxall[:, t * 8:(t + 1) * 8], "idxall", t * 32, (t + 1) * 32)
                for t in range(4)]
        e8 = kb.sb("e8", [128, 8], F32)
        w8all = kb.sb("w8all", [128, 32], F32)
        w8a = [kb.reg(f"w8a{t}", w8all[:, t * 8:(t + 1) * 8], "w8all", t * 32, (t + 1) * 32)
               for t in range(4)]
        colsT = kb.sb("colsT", [128, 16], F32)
        colsAP = colsT
        def _col(i, name):
            return kb.reg(name, colsAP[:, i:i + 1], "colsT", i * 4, (i + 1) * 4)
        sm1 = _col(0, "sm1")
        sq1 = _col(1, "sq1")
        mcol = _col(2, "mcol")
        mnegl = _col(3, "mnegl")
        vcol = _col(4, "vcol")
        rstd = _col(5, "rstd")
        mbias = _col(6, "mbias")
        mneg = _col(7, "mneg")
        ssum = _col(8, "ssum")
        epsc = _col(9, "epsc")
        srec = _col(10, "srec")
        colsB = kb.sb("colsB", [128, 8], F32)
        colsC = kb.sb("colsC", [128, 8], F32)
        rstd_t = [kb.reg(f"rstd{t}", colsB[:, t:t + 1], "colsB", t * 4, (t + 1) * 4) for t in range(4)]
        mcol_t = [kb.reg(f"mcol{t}", colsB[:, 4 + t:5 + t], "colsB", 16 + t * 4, 20 + t * 4) for t in range(4)]

        pb = [kb.reg(f"pb{i}", e(nc.psum_tensor(f"pb{i}", [128, 512], F32))[:], f"pb{i}", 0, 2048)
              for i in range(4)]
        pbf = kb.reg("pbf", e(nc.psum_tensor("pbf", [128, 512], F32))[:], "pbf", 0, 2048)
        pt2 = kb.reg("pt2", e(nc.psum_tensor("pt2", [2, 512], F32))[:], "pt2", 0, 2048)
        ptrf = [e(nc.psum_tensor(f"ptr{i}", [128, 512], F32))[:] for i in range(2)]
        ptr = [kb.reg(f"ptr{i}", ptrf[i].bitcast(BF16), f"ptr{i}", 0, 2048) for i in range(2)]

        kb.op("dve", lambda q: q.memset(xl13, 0.0), writes=["xl13"])
        kb.op("dve", lambda q: q.memset(gl2, 0.0), writes=["gl2"])
        kb.op("dve", lambda q: q.memset(epsc, EPS), writes=["epsc"])
        kb.dma("sp", "ini_a", lambda q: q.dma_start(out=rowb_s, in_=rowb_d), writes=["rowb_s"])
        kb.dma("sp", "ini_b", lambda q: q.dma_start(out=idb, in_=idn_d), writes=["idb"])
        kb.dma("pool", "ini_c", lambda q: q.dma_start(out=a13c, in_=a13_d), writes=["a13c"])
        kb.dma("pool", "ini_d", lambda q: q.dma_start(out=a2c, in_=a2_d), writes=["a2c"])
        CPW = 4 * 2304
        cast_q = []
        for j in range(22):
            sl = slice(j * CPW, (j + 1) * CPW)
            kb.dram_region(f"wxcp{j}", "wxc", sl.start * 2, sl.stop * 2)
            cast_q.append((lambda q, sl=sl: q.dma_start(out=wxc_d[:, sl], in_=wx_d[:, sl]),
                           f"wxcp{j}"))
        for i in range(NBLK):
            sl = slice(i * KW2 * 512, (i + 1) * KW2 * 512)
            kb.dram_region(f"w2cp{i}", "w2c", sl.start * 2, sl.stop * 2)
            cast_q.append((lambda q, sl=sl: q.dma_start(out=w2c_d[:, sl], in_=w2x_d[:, sl]),
                           f"w2cp{i}"))
        _ci = [0]

        def drip_cast(k):
            while k > 0 and _ci[0] < len(cast_q):
                fn, wname = cast_q[_ci[0]]
                _ci[0] += 1
                k -= 1
                kb.dma("pool", "cast", fn, writes=[wname])

        xnT3 = xnT.rearrange("p (k t) -> p k t", t=512)
        xfT3 = xfT.rearrange("p (k t) -> p k t", t=512)
        Pb3 = Pb.rearrange("p (k t) -> p k t", t=512)
        gf3 = gfold.rearrange("p (k t) -> p k t", t=512)
        wq3 = [w.rearrange("p (a k m) -> p a k m", a=2, m=128) for w in wqb]
        w8h3 = [w.rearrange("p (k t) -> p k t", t=512) for w in w8h]

        stats_done = [[False] * NTC for _ in range(nsub)]
        loads_done = [[False] * NTC for _ in range(nsub)]
        cores_done = [[False] * NTC for _ in range(nsub)]

        def load_task(s_, t_):
            base_ = s_ * SUB
            xslot = f"xin{t_ % 2}"
            kb.dma("pool", xslot, lambda q, r=base_ + t_ * 128, j=t_ % 2: q.dma_start(
                out=xin[j], in_=xin_d[r:r + 128, :]), writes=[xslot])

        def stat_core(s_, t_, junk=None, junk_name=None):
            if junk is None:
                junk, junk_name = tmpA, "gtmp0"
            xslot = f"xin{t_ % 2}"
            xap = xin[t_ % 2]
            kb.op("dve", lambda q, a=xap: q.reduce_sum(sm1, a, axis=AX.X), reads=[xslot], writes=["sm1"])
            for j8 in range(8):
                kb.op("act", lambda q, a=xap, jj=j8, jk=junk: q.activation(
                    jk, a[:, jj * 512:(jj + 1) * 512],
                    AF.Square, accum_out=colsC[:, jj:jj + 1]),
                    reads=[xslot], writes=[junk_name, "colsC"])
            kb.op("dve", lambda q: q.reduce_sum(sq1, colsC, axis=AX.X), reads=["colsC"], writes=["sq1"])
            kb.op("dve", lambda q, tt=t_: q.tensor_scalar(mcol_t[tt], sm1, 1.0 / D, None, op0=ALU.mult),
                  reads=["sm1"], writes=[f"mcol{t_}"])
            kb.op("dve", lambda q, tt=t_: q.tensor_scalar(mnegl, mcol_t[tt], -1.0, None, op0=ALU.mult),
                  reads=[f"mcol{t_}"], writes=["mnegl"])
            kb.op("dve", lambda q: q.tensor_scalar(vcol, sq1, 1.0 / D, None, op0=ALU.mult),
                  reads=["sq1"], writes=["vcol"])
            kb.op("dve", lambda q, tt=t_: q.scalar_tensor_tensor(
                vcol, mcol_t[tt], mnegl, vcol, op0=ALU.mult, op1=ALU.add),
                reads=[f"mcol{t_}", "mnegl", "vcol"], writes=["vcol"])
            kb.op("act", lambda q: q.activation(vcol, vcol, AF.Ln, bias=epsc), reads=["vcol", "epsc"], writes=["vcol"])
            kb.op("act", lambda q, tt=t_: q.activation(rstd_t[tt], vcol, AF.Exp, scale=-0.5),
                  reads=["vcol"], writes=[f"rstd{t_}"])

        def norm_part(s_, t_):
            xslot = f"xin{t_ % 2}"
            xap = xin[t_ % 2]
            for sl4 in range(4):
                kb.op("dve", lambda q, a=xap, o=xnb[t_], tt=t_, ss=sl4: q.tensor_scalar(
                    o[:, ss * 1024:(ss + 1) * 1024], a[:, ss * 1024:(ss + 1) * 1024],
                    mcol_t[tt], rstd_t[tt], op0=ALU.subtract, op1=ALU.mult),
                    reads=[xslot, f"mcol{t_}", f"rstd{t_}"], writes=[f"xnb{t_}"])
            xfb = xftb[t_]
            xfn = f"xftb{t_}"
            kb.op("dve", lambda q, o=xnb[t_], w=xfb: q.tensor_add(w, o[:, 0:1024], o[:, 1024:2048]),
                  reads=[f"xnb{t_}"], writes=[xfn])
            kb.op("dve", lambda q, o=xnb[t_], w=xfb: q.tensor_add(w, w, o[:, 2048:3072]),
                  reads=[f"xnb{t_}", xfn], writes=[xfn])
            kb.op("dve", lambda q, o=xnb[t_], w=xfb: q.tensor_add(w, w, o[:, 3072:4096]),
                  reads=[f"xnb{t_}", xfn], writes=[xfn])
            kb.dma("pool", f"x2w{s_}_{t_}", lambda q, o=xnb[t_], d=xn2_d[(s_, t_)]: q.dma_start(
                out=d[:, 0:4096], in_=o), reads=[f"xnb{t_}"], writes=[f"xn2_{s_}_{t_}"])
            kb.dma("pool", f"x2w{s_}_{t_}", lambda q, o=xnb[t_], d=xn2_d[(s_, t_)]: q.dma_start(
                out=d[:, 4096:8192], in_=o), reads=[f"xnb{t_}"], writes=[f"xn2_{s_}_{t_}"])

        def stat_task(s_, t_):
            stats_done[s_][t_] = True
            load_task(s_, t_)
            stat_core(s_, t_, *((aXj, "aXj") if t_ == 1 else (None, None)))
            norm_part(s_, t_)

        def t_phase(t):
            kb.op("dve", lambda q, tt=t: q.max(out=max8b, in_=corr[tt]), reads=[f"corr{t}"], writes=["max8b"])
            kb.op("dve", lambda q, tt=t: q.max_index(out=d8, in_max=max8b, in_values=corr[tt]),
                  reads=[f"corr{t}", "max8b"], writes=["d8"])
            kb.op("dve", lambda q: q.tensor_copy(out=max8f, in_=max8b), reads=["max8b"], writes=["max8f"])
            kb.op("dve", lambda q: q.tensor_scalar(mneg, max8f[:, 0:1], -1.0, None, op0=ALU.mult),
                  reads=["max8f"], writes=["mneg"])
            kb.op("act", lambda q: q.activation(e8, max8f, AF.Exp, bias=mneg),
                  reads=["max8f", "mneg"], writes=["e8"])
            kb.op("dve", lambda q: q.reduce_sum(ssum, e8, axis=AX.X), reads=["e8"], writes=["ssum"])
            kb.op("dve", lambda q: q.reciprocal(srec, ssum), reads=["ssum"], writes=["srec"])
            kb.op("dve", lambda q, tt=t: q.tensor_scalar(w8a[tt], e8, srec, None, op0=ALU.mult),
                  reads=["e8", "srec"], writes=[f"w8a{t}"])
            kb.op("dve", lambda q, tt=t: q.tensor_tensor(
                idxa[tt], d8, rowb_s[:, 0:1].to_broadcast([128, 8]), op=ALU.add),
                reads=["d8", "rowb_s"], writes=[f"idxa{t}"])

        for s in range(nsub):
            base = s * SUB
            for t in range(NTC):
                if not stats_done[s][t]:
                    if not loads_done[s][t]:
                        load_task(s, t)
                    if not cores_done[s][t]:
                        stat_core(s, t)
                    norm_part(s, t)
                    stats_done[s][t] = True
                scr, sname2 = scrs[t % 2], f"xcs{t % 2}"
                kb.op("dve", lambda q, a=xnb[t], o=scr: q.tensor_add(
                    o[:, 0:2047], a[:, 1:2048], a[:, 4095:2048:-1]),
                    reads=[f"xnb{t}"], writes=[sname2])
                kb.op("dve", lambda q, a=xnb[t], o=scr: q.tensor_copy(
                    out=o[:, 2047:2048], in_=a[:, 0:1]), reads=[f"xnb{t}"], writes=[sname2])
                kb.op("dve", lambda q, a=xnb[t], o=scr: q.tensor_copy(
                    out=o[:, 2048:2049], in_=a[:, 2048:2049]), reads=[f"xnb{t}"], writes=[sname2])
                kb.op("dve", lambda q, o=scr: q.memset(o[:, 2049:2176], 0.0), writes=[sname2])
                kb.op("dve", lambda q, a=xnb[t], o=scr: q.tensor_tensor(
                    o[:, 2176:4223], a[:, 1:2048], a[:, 4095:2048:-1], op=ALU.subtract),
                    reads=[f"xnb{t}"], writes=[sname2])
                kb.op("dve", lambda q, o=scr: q.memset(o[:, 4223:4224], 0.0), writes=[sname2])
                for gi, (boff, bln) in enumerate([(0, 8), (8, 8), (16, 8), (24, 8), (32, 1)]):
                    pslot, pap = f"ptr{gi % 2}", ptr[gi % 2]
                    for j in range(bln):
                        fcn = boff + j
                        kb.op("pe", lambda q, src=scr, f=fcn, o=pap, jj=j: q.transpose(
                            o[:, jj * 128:(jj + 1) * 128], src[:, f * 128:(f + 1) * 128], idb),
                            reads=[sname2, "idb"], writes=[pslot])
                    dst = xnT3[:, boff:boff + bln, t * 128:(t + 1) * 128]
                    kb.op("act", lambda q, o=dst, i=pap, ln=bln: q.copy(
                        o, i[:, 0:ln * 128].rearrange("p (j m) -> p j m", m=128)),
                        reads=[pslot], writes=["xnT"])
                for j in range(8):
                    kb.op("pe", lambda q, f=j, o=ptr[0], jj=j, w=xftb[t]: q.transpose(
                        o[:, jj * 128:(jj + 1) * 128], w[:, f * 128:(f + 1) * 128], idb),
                        reads=[f"xftb{t}", "idb"], writes=["ptr0"])
                dst = xfT3[:, :, t * 128:(t + 1) * 128]
                kb.op("act", lambda q, o=dst, i=ptr[0]: q.copy(o, i.rearrange("p (j m) -> p j m", m=128)),
                      reads=["ptr0"], writes=["xfT"])

            for j in range(12):
                kb.dma("act", "dgA", lambda q, k=j: q.dma_start(
                    out=dti[k], in_=tabD_d[:, k * DC:(k + 1) * DC]), writes=[f"dti{j}"])
            drip_cast(6)
            for j in range(3):
                kb.dma("pool", "dgB", lambda q, k=j: q.dma_start(
                    out=dti2[k], in_=tabD_d[:, (12 + k) * DC:(13 + k) * DC]), writes=[f"dti2_{j}"])
            for kc in range(33):
                kb.op("pe", lambda q, k=kc: q.matmul(pbf[0:4, :], lhsT=a13c[:, k * 4:k * 4 + 4],
                                                     rhs=xnT[:, k * 512:(k + 1) * 512],
                                                     start=(k == 0), stop=(k == 32)),
                      reads=["a13c", "xnT"], writes=["pbf"])
            kb.op("act", lambda q: q.mul(xl13[0:4, :], pbf[0:4, :], 0.5), reads=["pbf"], writes=["xl13"])

            kb.dma("sp", "tabC0", lambda q: q.dma_start(out=tabC_s[0], in_=tabC_d[:, 0:2176]), writes=["tabC0"])
            kb.dma("sp", "tabS0", lambda q: q.dma_start(out=tabS_s[0], in_=tabS_d[:, 0:2048]), writes=["tabS0"])
            for fc in range(FC):
                if fc + 1 < FC:
                    j = (fc + 1) % 2
                    kb.dma("sp", f"tabC{j}", lambda q, f=fc + 1, jj=j: q.dma_start(
                        out=tabC_s[jj], in_=tabC_d[:, f * 2176:(f + 1) * 2176]), writes=[f"tabC{j}"])
                    kb.dma("sp", f"tabS{j}", lambda q, f=fc + 1, jj=j: q.dma_start(
                        out=tabS_s[jj], in_=tabS_d[:, f * 2048:(f + 1) * 2048]), writes=[f"tabS{j}"])
                cslot, sslot = f"tabC{fc % 2}", f"tabS{fc % 2}"
                cap, sap = tabC_s[fc % 2], tabS_s[fc % 2]
                rb, ib = f"pb{fc % 2}", f"pb{2 + fc % 2}"
                rap, iap = pb[fc % 2], pb[2 + fc % 2]
                for kc in range(17):
                    kb.op("pe", lambda q, k=kc, c=cap, o=rap: q.matmul(
                        o, lhsT=c[:, k * 128:(k + 1) * 128], rhs=xnT[:, k * 512:(k + 1) * 512],
                        start=(k == 0), stop=(k == 16)), reads=[cslot, "xnT"], writes=[rb])
                for kc in range(16):
                    kb.op("pe", lambda q, k=kc, c=sap, o=iap: q.matmul(
                        o, lhsT=c[:, k * 128:(k + 1) * 128], rhs=xnT[:, (17 + k) * 512:(18 + k) * 512],
                        start=(k == 0), stop=(k == 15)), reads=[sslot, "xnT"], writes=[ib])
                kb.op("act", lambda q, i=rap: q.activation(tmpA, i, AF.Square), reads=[rb], writes=["gtmp0"])
                kb.op("act", lambda q, i=iap: q.activation(tmpB, i, AF.Square), reads=[ib], writes=["gtmp1"])
                kb.op("dve", lambda q, f=fc: q.tensor_add(Pb[:, f * 512:(f + 1) * 512], tmpA, tmpB),
                      reads=["gtmp0", "gtmp1"], writes=["P"])
            for j in range(3, 5):
                kb.dma("pool", "dgB", lambda q, k=j: q.dma_start(
                    out=dti2[k], in_=tabD_d[:, (12 + k) * DC:(13 + k) * DC]), writes=[f"dti2_{j}"])

            for j0 in range(2):
                kb.dma(("sp", "act")[j0 % 2], f"wq{j0}", lambda q, i=j0: q.dma_start(
                    out=wqb[i], in_=wxc_d[:, i * 2304:(i + 1) * 2304]),
                    reads=["wxcp0"], writes=[f"wq{j0}"])
            for t in range(NTC):
                for n in range(NBC):
                    bank = (t * NBC + n) % 4
                    bap, bname = pb[bank], f"pb{bank}"
                    nw = 512 if n < 4 else 1
                    for kc in range(FC):
                        dd = dti[kc] if kc < 12 else dti2[kc - 12]
                        dname = f"dti{kc}" if kc < 12 else f"dti2_{kc - 12}"
                        kb.op("pe", lambda q, k=kc, tt=t, nn=n, o=bap, w=nw, d_=dd: q.matmul(
                            o[:, 0:w], lhsT=Pb3[:, k, tt * 128:(tt + 1) * 128],
                            rhs=d_[:, nn * 512:nn * 512 + w],
                            start=(k == 0), stop=(k == FC - 1)),
                            reads=["P", dname], writes=[bname])
                    kb.op("act", lambda q, tt=t, nn=n, i=bap, w=nw: q.copy(
                        corr[tt][:, nn * 512:nn * 512 + w], i[:, 0:w]),
                        reads=[bname], writes=[f"corr{t}"])
                kb.op("dve", lambda q, tt=t: q.tensor_copy(
                    out=corr[tt][:, 2049:4096], in_=corr[tt][:, 2047:0:-1]),
                    reads=[f"corr{t}"], writes=[f"corr{t}"])
                t_phase(t)
                drip_cast(1)

            def mac_task(t, j):
                gslot, gap = f"gout{(t * 8 + j) % 2}", gout[(t * 8 + j) % 2]
                kb.dma("pool", f"g{(t * 8 + j) % 2}", lambda q, jj=j, o=gap, d=xn2_d[(s, t)], tt=t:
                       q.indirect_dma_start(
                           out=o, out_offset=None, in_=d,
                           in_offset=IndirectOffsetOnAxis(idxa[tt][:, jj:jj + 1], axis=1)),
                       reads=[f"idxa{t}", f"xn2_{s}_{t}"], writes=[gslot])
                sname = f"sfull{t}"
                if j == 0:
                    kb.op("dve", lambda q, g=gap, tt=t: q.tensor_scalar(
                        s_full[tt], g, w8a[tt][:, 0:1], None, op0=ALU.mult),
                        reads=[gslot, f"w8a{t}"], writes=[sname])
                else:
                    kb.op("dve", lambda q, g=gap, jj=j, tt=t: q.scalar_tensor_tensor(
                        s_full[tt], g, w8a[tt][:, jj:jj + 1], s_full[tt],
                        op0=ALU.mult, op1=ALU.add),
                        reads=[gslot, f"w8a{t}", sname], writes=[sname])

            mac_tasks = [(t, j) for t in range(NTC) for j in range(8)]
            mac_i = 0

            def emit_t2_fold(cc):
                for g in range(4):
                    qi = cc * 4 + g
                    gs = 4 * (cc % 2) + g
                    kb.op("pe", lambda q, i=qi, gg=gs: q.matmul(
                        pt2[:], lhsT=a2c[:, i * 2:(i + 1) * 2], rhs=gated[gg],
                        start=(i == 0), stop=(i == HQ - 1)),
                        reads=["a2c", f"gated{gs}"], writes=["pt2"])
                g0, g1, g2, g3 = (4 * (cc % 2) + g for g in range(4))
                kb.op("dve", lambda q, a=gated[g0], b=gated[g1]: q.tensor_add(foldt, a, b),
                      reads=[f"gated{g0}", f"gated{g1}"], writes=["foldt"])
                kb.op("dve", lambda q, a=gated[g2]: q.tensor_add(foldt, foldt, a),
                      reads=[f"gated{g2}", "foldt"], writes=["foldt"])
                kb.op("dve", lambda q, a=gated[g3], cc=cc: q.tensor_add(
                    gfold[:, cc * 512:(cc + 1) * 512], foldt, a),
                    reads=[f"gated{g3}", "foldt"], writes=["gfold"])

            for c in range(GF):
                for g in range(4):
                    qi = c * 4 + g
                    slot = qi % 3
                    gs = 4 * (c % 2) + g
                    if qi + 2 < HQ:
                        ns = (qi + 2) % 3
                        kb.dma(("sp", "act")[(qi + 2) % 2], f"wq{ns}", lambda q, i=qi + 2, jj=ns: q.dma_start(
                            out=wqb[jj], in_=wxc_d[:, i * 2304:(i + 1) * 2304]),
                            reads=[f"wxcp{(qi + 2) // 4}"], writes=[f"wq{ns}"])
                    h1n, h3n = f"pb{2 * (qi % 2)}", f"pb{2 * (qi % 2) + 1}"
                    h1ap, h3ap = pb[2 * (qi % 2)], pb[2 * (qi % 2) + 1]
                    wslot, wap = f"wq{slot}", wq3[slot]
                    for kc in range(KW1):
                        rhs = xfT[:, kc * 512:(kc + 1) * 512] if kc < 8 else xl13
                        rname = "xfT" if kc < 8 else "xl13"
                        kb.op("pe", lambda q, k=kc, w=wap, o=h1ap, r=rhs: q.matmul(
                            o, lhsT=w[:, 0, k, :], rhs=r, start=(k == 0), stop=(k == KW1 - 1)),
                            reads=[wslot, rname], writes=[h1n])
                    for kc in range(KW1):
                        rhs = xfT[:, kc * 512:(kc + 1) * 512] if kc < 8 else xl13
                        rname = "xfT" if kc < 8 else "xl13"
                        kb.op("pe", lambda q, k=kc, w=wap, o=h3ap, r=rhs: q.matmul(
                            o, lhsT=w[:, 1, k, :], rhs=r, start=(k == 0), stop=(k == KW1 - 1)),
                            reads=[wslot, rname], writes=[h3n])
                    if g == 3 and c >= 1:
                        emit_t2_fold(c - 1)
                    kb.op("act", lambda q, i=h1ap, o=gated[gs]: q.activation(o, i, AF.Silu),
                          reads=[h1n], writes=[f"gated{gs}"])
                    kb.op("dve", lambda q, gg=gs, hp=h3ap: q.tensor_mul(gated[gg], gated[gg], hp),
                          reads=[f"gated{gs}", h3n], writes=[f"gated{gs}"])
                    if c >= 2 and g % 2 == 1 and mac_i < len(mac_tasks):
                        mac_task(*mac_tasks[mac_i])
                        mac_i += 1
                    if g == 3:
                        drip_cast(2)
                    if g == 3 and c in (19, 21) and s + 1 < nsub and not stats_done[s + 1][0 if c == 19 else 1]:
                        stat_task(s + 1, 0 if c == 19 else 1)
            emit_t2_fold(GF - 1)
            while mac_i < len(mac_tasks):
                mac_task(*mac_tasks[mac_i])
                mac_i += 1
            kb.op("act", lambda q: q.mul(gl2[0:2, :], pt2[:], 0.5), reads=["pt2"], writes=["gl2"])

            def w2piece(n, p):
                off = n * KW2 * 512 + 6 * p * 512
                ln = (6 if p < 3 else 5) * 512
                return off, ln
            pieces = [(n, p) for n in range(NBLK) for p in range(4)]
            for pre in range(3):
                n2, p2 = pieces[pre]
                off, ln = w2piece(n2, p2)
                kb.dma(("act", "sp")[pre % 2], f"w8h{pre}", lambda q, o=off, l=ln, jj=pre: q.dma_start(
                    out=w8h[jj][:, 0:l], in_=w2c_d[:, o:o + l]), reads=[f"w2cp{n2}"], writes=[f"w8h{pre}"])
            ep = 0
            for pi, (n, p) in enumerate(pieces):
                slot = pi % 4
                if pi + 3 < len(pieces):
                    n2, p2 = pieces[pi + 3]
                    off, ln = w2piece(n2, p2)
                    ns = (pi + 3) % 4
                    kb.dma(("act", "sp")[(pi + 3) % 2], f"w8h{ns}", lambda q, o=off, l=ln, jj=ns: q.dma_start(
                        out=w8h[jj][:, 0:l], in_=w2c_d[:, o:o + l]), reads=[f"w2cp{n2}"], writes=[f"w8h{ns}"])
                kcs = range(6 * p, min(6 * p + 6, KW2))
                for t in range(NTC):
                    bank = (n * 4 + t) % 5
                    bap, bname = (pb + [pbf])[bank], (["pb0", "pb1", "pb2", "pb3", "pbf"])[bank]
                    for kc in kcs:
                        if kc < GF:
                            lhs, lname = gf3[:, kc, t * 128:(t + 1) * 128], "gfold"
                        else:
                            lhs, lname = gl2[:, t * 128:(t + 1) * 128], "gl2"
                        ko = kc - 6 * p
                        kb.op("pe", lambda q, k=ko, o=bap, l=lhs, w=w8h3[slot], st=(kc == 0), sp=(kc == KW2 - 1): q.matmul(
                            o, lhsT=l, rhs=w[:, k, :], start=st, stop=sp),
                            reads=[lname, f"w8h{slot}"], writes=[bname])
                if p == 3:
                    if s + 1 < nsub:
                        if n == 0 and not loads_done[s + 1][2]:
                            loads_done[s + 1][2] = True
                            load_task(s + 1, 2)
                        if n == 1 and not cores_done[s + 1][2]:
                            cores_done[s + 1][2] = True
                            stat_core(s + 1, 2, aXj, "aXj")
                        if n == 2 and not stats_done[s + 1][2]:
                            stats_done[s + 1][2] = True
                            norm_part(s + 1, 2)
                            if not loads_done[s + 1][3]:
                                loads_done[s + 1][3] = True
                                load_task(s + 1, 3)
                        if n == 3 and not cores_done[s + 1][3]:
                            cores_done[s + 1][3] = True
                            stat_core(s + 1, 3, aXj, "aXj")
                        if n == 4 and not stats_done[s + 1][3]:
                            stats_done[s + 1][3] = True
                            norm_part(s + 1, 3)
                    for t in range(NTC):
                        bank = (n * 4 + t) % 5
                        bap, bname = (pb + [pbf])[bank], (["pb0", "pb1", "pb2", "pb3", "pbf"])[bank]
                        gsl, gapp = gtmp_names[ep % 2], gtmp[ep % 2]
                        kb.op("act", lambda q, i=bap, o=gapp: q.activation(o, i, AF.Silu),
                              reads=[bname], writes=[gsl])
                        xs, xa = f"xepi{ep % 2}", xepi[ep % 2]
                        r0 = base + t * 128
                        kb.dma("pool", xs, lambda q, r=r0, nn=n, o=xa: q.dma_start(
                            out=o, in_=xin_d[r:r + 128, nn * 512:(nn + 1) * 512]), writes=[xs])
                        sname = f"s_{t}_{n}"
                        kb.op("dve", lambda q, tt=t, nn=n, g=gapp: q.tensor_mul(s_sl[tt][nn], s_sl[tt][nn], g),
                              reads=[sname, gsl], writes=[sname])
                        kb.op("dve", lambda q, tt=t, nn=n, x=xa: q.tensor_add(s_sl[tt][nn], s_sl[tt][nn], x),
                              reads=[sname, xs], writes=[sname])
                        kb.dma("pool", "outw", lambda q, r=r0, tt=t, nn=n: q.dma_start(
                            out=out_d[r:r + 128, nn * 512:(nn + 1) * 512], in_=s_sl[tt][nn]),
                            reads=[sname])
                        ep += 1

        kb.finalize_drains()
        block = e(nc.Block())
        kb.emit(block)
    return nc


from concourse.bass_utils import run_bass_kernel_spmd

_NCORES = 8


def kernel(L0_attn_x, L0_ffn_w1q, L0_ffn_a1, L0_ffn_b1, L0_ffn_w3q, L0_ffn_a3,
           L0_ffn_b3, L0_ffn_w2q, L0_ffn_a2, L0_ffn_b2):
    x = np.ascontiguousarray(np.asarray(L0_attn_x, dtype=np.float32).reshape(16384, D))
    hp = host_prepare(L0_ffn_w1q, L0_ffn_a1, L0_ffn_b1, L0_ffn_w3q, L0_ffn_a3,
                      L0_ffn_b3, L0_ffn_w2q, L0_ffn_a2, L0_ffn_b2)
    shards = np.split(x, _NCORES, axis=0)
    in_maps = [dict(x=shards[i], **hp) for i in range(_NCORES)]
    nc = build_program(nsub=4)
    res = run_bass_kernel_spmd(nc, in_maps, core_ids=list(range(_NCORES)))
    out = np.concatenate([res.results[i]["out"] for i in range(_NCORES)], axis=0)
    return out.reshape(2, 8192, D)
